# version 04
# baseline (speedup 1.0000x reference)
import numpy as np
import ml_dtypes
import concourse.bass as bass
import concourse.mybir as mybir
from concourse.alu_op_type import AluOpType as OP
from concourse.bass_utils import run_bass_kernel_spmd

F32 = mybir.dt.float32
BF16 = mybir.dt.bfloat16
I32 = mybir.dt.int32
AF = mybir.ActivationFunctionType
AX = mybir.AxisListType
BF16NP = ml_dtypes.bfloat16

T = 2048
D = 1024
NJ = D // 128
TC = 512
NTC = T // TC

ALL_STAGES = ["ffn0a", "attn0", "ffn0b", "ffn1a", "mamba1", "ffn1b"]
STAGE_SETS = [ALL_STAGES]


class Sem:
    def __init__(self, handle):
        self.h = handle
        self.n = 0

    def bump(self, k=1):
        self.n += k
        return self.n

    def at(self):
        return (self, self.n)


class B:
    ENGS = ("sync", "scalar", "vector", "tensor", "gpsimd")

    def __init__(self, nc):
        self.nc = nc
        self.streams = {e: [] for e in self.ENGS}
        self._guards = []
        self.bar = None
        self._uid = 0

    def sem(self, name):
        g = self.nc.semaphore(name)
        h = g.__enter__()
        self._guards.append(g)
        return Sem(h)

    def sbuf(self, name, shape, dtype):
        self._uid += 1
        g = self.nc.sbuf_tensor(f"{name}_{self._uid}", shape, dtype)
        return g.__enter__(), g

    def psum(self, name, shape, dtype=F32):
        self._uid += 1
        g = self.nc.psum_tensor(f"{name}_{self._uid}", shape, dtype)
        return g.__enter__(), g

    def free(self, *guards):
        for g in guards:
            g.__exit__(None, None, None)

    def op(self, eng, fn, waits=(), inc=None, k=1):
        ws = [(s.h, n) for (s, n) in waits]
        ih = ik = None
        if inc is not None:
            inc.bump(k)
            ih, ik = inc.h, k

        def run(e, ws=ws, fn=fn, ih=ih, ik=ik):
            for h, n in ws:
                e.wait_ge(h, n)
            ins = fn(e)
            if ih is not None:
                ins.then_inc(ih, ik)
        self.streams[eng].append(run)

    def wait(self, eng, *waits):
        ws = [(s.h, n) for (s, n) in waits]

        def run(e, ws=ws):
            for h, n in ws:
                e.wait_ge(h, n)
        self.streams[eng].append(run)

    def barrier(self):
        if self.bar is None:
            self.bar = self.sem("bar")
        self.bar.n += 5
        target = self.bar.n

        def mk(e, target=target):
            e.sem_inc(self.bar.h, 1)
            e.wait_ge(self.bar.h, target)
        for eng in self.ENGS:
            self.streams[eng].append(mk)

    def finalize(self):
        nc = self.nc
        with nc.Block() as block:
            @block.sync
            def _(e):
                for fn in self.streams["sync"]:
                    fn(e)

            @block.scalar
            def _(e):
                for fn in self.streams["scalar"]:
                    fn(e)

            @block.vector
            def _(e):
                for fn in self.streams["vector"]:
                    fn(e)

            @block.tensor
            def _(e):
                for fn in self.streams["tensor"]:
                    fn(e)

            @block.gpsimd
            def _(e):
                for fn in self.streams["gpsimd"]:
                    fn(e)
        for g in reversed(self._guards):
            g.__exit__(None, None, None)


_VEC_ORDER = [
    ("ones", 128),
    ("eps", 1),
    ("kinv1", 1),
    ("kh2", 1),
    ("b1", 32), ("b2", 8),
    ("bq", 8), ("bk", 2), ("bo", 8),
    ("gam", 8), ("bet", 8),
    ("fc1b", 32), ("fc2b", 8),
    ("cw", 64), ("convb", 16), ("dtb", 16), ("dd", 16),
    ("mg", 8), ("mb", 8),
    ("alog", 256),
    ("hsel", 4), ("id4", 4),
]
VEC_COLS = {}
_c = 0
for _nm, _n in _VEC_ORDER:
    VEC_COLS[_nm] = (_c, _n)
    _c += _n
VEC_TOT = _c


def perm_q(x):
    idx = []
    for h in range(16):
        idx += [64 * h + 2 * j for j in range(32)]
    for h in range(16):
        idx += [64 * h + 2 * j + 1 for j in range(32)]
    return np.ascontiguousarray(np.take(x, np.array(idx), axis=-1))


def perm_k(x):
    idx = []
    for h in range(4):
        idx += [64 * h + 2 * j for j in range(32)]
    for h in range(4):
        idx += [64 * h + 2 * j + 1 for j in range(32)]
    return np.ascontiguousarray(np.take(x, np.array(idx), axis=-1))


def build_vec_blob(inp):
    v = np.zeros((128, VEC_TOT), np.float32)

    def put(name, arr):
        c0, n = VEC_COLS[name]
        v[:, c0:c0 + n] = arr

    def pt(x):
        n = x.shape[0] // 128
        return np.ascontiguousarray(x.reshape(n, 128).T)

    put("ones", np.ones((128, 128), np.float32))
    put("eps", np.full((128, 1), 1e-5, np.float32))
    put("kinv1", np.full((128, 1), 1.0 / 1024.0, np.float32))
    put("kh2", np.full((128, 1), 0.5 / 4096.0, np.float32))
    put("b1", pt(np.asarray(inp["L0_ffn_b1"])))
    put("b2", pt(np.asarray(inp["L0_ffn_b2"])))
    put("bq", pt(perm_q(np.asarray(inp["L0_attn_bq"]))))
    put("bk", pt(perm_k(np.asarray(inp["L0_attn_bk"]))))
    put("bo", pt(np.asarray(inp["L0_attn_bo"])))
    put("gam", pt(np.asarray(inp["L1_ffn_ln_gamma"])))
    put("bet", pt(np.asarray(inp["L1_ffn_ln_beta"])))
    put("fc1b", pt(np.asarray(inp["L1_ffn_fc1_bias"])))
    put("fc2b", pt(np.asarray(inp["L1_ffn_fc2_bias"])))
    cw = np.asarray(inp["L1_attn_conv_w"]).reshape(16, 128, 4)
    put("cw", np.ascontiguousarray(cw.transpose(1, 0, 2).reshape(128, 64)))
    put("convb", pt(np.asarray(inp["L1_attn_conv_b"])))
    put("dtb", pt(np.asarray(inp["L1_attn_dt_proj_b"])))
    put("dd", pt(np.asarray(inp["L1_attn_D"])))
    put("mg", pt(np.asarray(inp["L1_attn_ln_g"])))
    put("mb", pt(np.asarray(inp["L1_attn_ln_b"])))
    al = np.asarray(inp["L1_attn_A_log"]).reshape(16, 128, 16)
    put("alog", np.ascontiguousarray(al.transpose(1, 0, 2).reshape(128, 256)))
    hs = np.zeros((128, 4), np.float32)
    for p in range(128):
        hs[p, p // 32] = 1.0
    put("hsel", hs)
    i4 = np.zeros((128, 4), np.float32)
    i4[0:4, 0:4] = np.eye(4)
    put("id4", i4)
    return v


class Kern:
    def __init__(self, stages):
        self.stages = stages
        nc = bass.Bass("TRN2", target_bir_lowering=False)
        self.nc = nc
        self.b = B(nc)
        self.xin = nc.dram_tensor("xin", [D, T], F32, kind="ExternalInput").ap()
        self.xout = nc.dram_tensor("xout", [D, T], F32, kind="ExternalOutput").ap()
        self.xmid = nc.dram_tensor("xmid", [D, T], F32, kind="Internal").ap()
        self.xnspill = nc.dram_tensor("xnspill", [D, T], BF16, kind="Internal").ap()
        self.xnmsp = nc.dram_tensor("xnmsp", [D, T], BF16, kind="Internal").ap()
        self.vec_h = nc.dram_tensor("vec", [128, VEC_TOT], F32, kind="ExternalInput").ap()
        self.dbg = nc.dram_tensor("dbg", [128, 32 * TC], F32, kind="ExternalOutput").ap()
        self.dbg_used = False
        w = {}

        def dram(name, shape, dt):
            w[name] = nc.dram_tensor(name, shape, dt, kind="ExternalInput").ap()
        if any(s.startswith("ffn0") for s in stages):
            dram("w1T", [D, 4096], F32)
            dram("w2T", [4096, D], F32)
        if "attn0" in stages:
            dram("wq", [D, D], F32)
            dram("wk", [D, 256], F32)
            dram("wv", [D, 256], F32)
            dram("bv", [1, 256], F32)
            dram("wo", [D, D], F32)
            dram("rotq", [128, 2 * T], F32)
            dram("rotk", [128, 2 * T], F32)
            dram("cmask", [128, 4 * TC], BF16)
            dram("identb", [128, 128], BF16)
            dram("onesbf", [1, 128], BF16)
        if "mamba1" in stages:
            dram("ipw", [D, 4096], F32)
            dram("xpw", [2048, 96], F32)
            dram("dpw", [64, 2048], F32)
            dram("opw", [2048, D], F32)
            dram("rs", [1, 1], F32)
            self.ipwc = nc.dram_tensor("ipwc", [D, 4096], BF16, kind="Internal").ap()
            self.opwc = nc.dram_tensor("opwc", [2048, D], BF16, kind="Internal").ap()
        if any(s.startswith("ffn1") for s in stages):
            dram("q1", [D, 4096], I32)
            dram("q2", [4096, D], I32)
            dram("sc1", [8, 4096], F32)
            dram("sc2", [32, D], F32)
            dram("rowsel8", [8, 8 * 128], F32)
        if any(s.startswith("ffn1") for s in stages):
            dram("rowsel32", [32, 32 * 128], F32)
        self.w = w

        b = self.b
        self.vec, self._vg = b.sbuf("vecsb", [128, VEC_TOT], F32)
        self.s_ld = b.sem("s_ld")
        self.s_gld = b.sem("s_gld")
        self.s_st = b.sem("s_st")
        self.s_t = b.sem("s_t")
        self.s_v = b.sem("s_v")
        self.s_s = b.sem("s_s")
        self.s_x = b.sem("s_x")
        self.sx2 = [b.sem("sx2a"), b.sem("sx2b")]
        self.sw2 = [b.sem("sw2a"), b.sem("sw2b")]
        self.sq3 = [b.sem(f"sq3{i}") for i in range(3)]
        self.s_sp1 = b.sem("s_sp1")
        self.s_sp2 = b.sem("s_sp2")

    def vsl(self, name, c0=0, n=None):
        base, tot = VEC_COLS[name]
        if n is None:
            n = tot
        return self.vec[:, base + c0: base + c0 + n]

    def emit(self):
        b = self.b
        b.op("sync", lambda e: e.dma_start(out=self.vec[:], in_=self.vec_h),
             inc=self.s_ld, k=16)
        b.barrier()
        cur_in = self.xin
        for i, st in enumerate(self.stages):
            last = (i == len(self.stages) - 1)
            cur_out = self.xout if last else self.xmid
            if st in ("ffn0a", "ffn0b"):
                self.emit_ffn0(cur_in, cur_out)
            elif st in ("ffn1a", "ffn1b"):
                self.emit_ffn1(cur_in, cur_out)
            elif st == "attn0":
                self.emit_attn(cur_in, cur_out)
            elif st == "mamba1":
                self.emit_mamba(cur_in, cur_out)
            else:
                raise ValueError(st)
            b.barrier()
            cur_in = cur_out
        b.wait("sync", self.s_st.at())
        b.finalize()

    def emit_ln(self, xsrc, xn_sb, affine=None, spill=None, affine_spill=None):
        b = self.b
        xt, g1 = b.sbuf("lnx", [128, 2 * T], F32)
        sq, g2 = b.sbuf("lnsq", [128, 2 * T], F32)
        rows, g3 = b.sbuf("lnrows", [1, 8 * T], F32)
        bc, g4 = b.sbuf("lnbc", [128, 3 * T], F32)
        t1s, g5 = b.sbuf("lnt1", [128, T], F32)
        xm, g6 = b.sbuf("lnxm", [128, T], BF16)
        ones1 = self.vsl("ones", 0, 1)
        onesr = self.vec[0:1, 0:128]
        eps_r = self.vec[0:1, VEC_COLS["eps"][0]:VEC_COLS["eps"][0] + 1]

        def row(r):
            return rows[0:1, r * T:(r + 1) * T]

        def rowc(r, cs):
            return rows[0:1, r * T:(r + 1) * T][:, cs]

        ps_sx, gpx = b.psum("lnpsx", [1, T])
        ps_sq, gpq = b.psum("lnpsq", [1, T])
        sq_done = [0] * NJ
        mm_done = [0] * NJ
        xld = [0] * NJ
        for j in range(NJ):
            wl = []
            if j >= 2:
                wl = [(self.s_s, sq_done[j - 2]), (self.s_t, mm_done[j - 2])]
            b.op("sync", lambda e, j=j: e.dma_start(
                out=xt[:, (j % 2) * T:(j % 2 + 1) * T],
                in_=xsrc[j * 128:(j + 1) * 128, :]), waits=wl,
                inc=self.sx2[j % 2], k=16)
            xld[j] = self.sx2[j % 2].n
            wl = [self.sx2[j % 2].at()]
            if j >= 2:
                wl.append((self.s_t, mm_done[j - 2]))
            b.op("scalar", lambda e, j=j: e.activation(
                out=sq[:, (j % 2) * T:(j % 2 + 1) * T],
                in_=xt[:, (j % 2) * T:(j % 2 + 1) * T], func=AF.Square),
                waits=wl, inc=self.s_s)
            sq_done[j] = self.s_s.n
            for c in range(NTC):
                cs = slice(c * TC, (c + 1) * TC)
                b.op("tensor", lambda e, j=j, cs=cs: e.matmul(
                    ps_sx[0:1, cs], ones1,
                    xt[:, (j % 2) * T:(j % 2 + 1) * T][:, cs],
                    start=(j == 0), stop=(j == NJ - 1)),
                    waits=[(self.sx2[j % 2], xld[j])] if c == 0 else [])
                b.op("tensor", lambda e, j=j, cs=cs: e.matmul(
                    ps_sq[0:1, cs], ones1,
                    sq[:, (j % 2) * T:(j % 2 + 1) * T][:, cs],
                    start=(j == 0), stop=(j == NJ - 1)),
                    waits=[(self.s_s, sq_done[j])] if c == 0 else [],
                    inc=self.s_t if c == NTC - 1 else None)
            mm_done[j] = self.s_t.n
        stats_done = self.s_t.n

        b.op("vector", lambda e: e.tensor_scalar(
            out=row(0), in0=ps_sx[0:1, :], scalar1=1.0 / D, scalar2=None,
            op0=OP.mult), waits=[(self.s_t, stats_done)])
        b.op("vector", lambda e: e.tensor_scalar(
            out=row(1), in0=ps_sq[0:1, :], scalar1=1.0 / D, scalar2=None,
            op0=OP.mult))
        b.op("vector", lambda e: e.tensor_tensor(
            out=row(2), in0=row(0), in1=row(0), op=OP.mult))
        b.op("vector", lambda e: e.tensor_tensor(
            out=row(1), in0=row(1), in1=row(2), op=OP.subtract),
            inc=self.s_v)
        b.op("scalar", lambda e: e.activation(
            out=row(2), in_=row(1), func=AF.Sqrt, bias=eps_r),
            waits=[self.s_v.at()], inc=self.s_s)
        b.op("vector", lambda e: e.reciprocal(out=row(3), in_=row(2)),
             waits=[self.s_s.at()], inc=self.s_v)
        if affine is not None:
            b.op("vector", lambda e: e.tensor_tensor(
                out=row(2), in0=row(1), in1=row(3), op=OP.mult))
            b.op("vector", lambda e: e.tensor_tensor(
                out=row(2), in0=row(2), in1=row(3), op=OP.mult),
                inc=self.s_v)
            b.op("scalar", lambda e: e.activation(
                out=row(5), in_=row(2), func=AF.Sqrt, bias=eps_r),
                waits=[self.s_v.at()], inc=self.s_s)
            b.op("vector", lambda e: e.reciprocal(out=row(6), in_=row(5)),
                 waits=[self.s_s.at()])
            b.op("vector", lambda e: e.tensor_tensor(
                out=row(4), in0=row(3), in1=row(6), op=OP.mult),
                inc=self.s_v)
        rows_done = self.s_v.n
        b.free(gpq, gpx)

        psbc, gpb = b.psum("lnpsb", [128, TC])
        nb = 3 if affine is not None else 2
        ev_done = 0
        for i in range(nb):
            ridx = (0, 3, 4)[i]
            for c in range(NTC):
                cs = slice(c * TC, (c + 1) * TC)
                b.op("tensor", lambda e, ridx=ridx, cs=cs: e.matmul(
                    psbc[:, :], onesr, rowc(ridx, cs), start=True, stop=True),
                    waits=[(self.s_v, max(rows_done, ev_done))], inc=self.s_t)
                b.op("vector", lambda e, i=i, cs=cs: e.tensor_copy(
                    out=bc[:, i * T:(i + 1) * T][:, cs], in_=psbc[:, :]),
                    waits=[self.s_t.at()], inc=self.s_v)
                ev_done = self.s_v.n
        bcast_done = self.s_v.n
        b.free(gpb)

        gcol, bcol = (None, None)
        if affine is not None:
            gcol, bcol = affine
        v_used = [0] * NJ
        spill_done = [0] * NJ
        aff_done = [0] * NJ
        for j in range(NJ):
            wl = [(self.s_v, bcast_done)]
            if j >= 2:
                wl.append((self.s_v, v_used[j - 2]))
            b.op("sync", lambda e, j=j: e.dma_start(
                out=xt[:, (j % 2) * T:(j % 2 + 1) * T],
                in_=xsrc[j * 128:(j + 1) * 128, :]), waits=wl,
                inc=self.sx2[j % 2], k=16)
            b.op("vector", lambda e, j=j: e.tensor_tensor(
                out=t1s[:, :], in0=xt[:, (j % 2) * T:(j % 2 + 1) * T],
                in1=bc[:, 0:T], op=OP.subtract), waits=[self.sx2[j % 2].at()])
            if affine is None:
                b.op("vector", lambda e, j=j: e.tensor_tensor(
                    out=xn_sb[:, j * T:(j + 1) * T], in0=t1s[:, :],
                    in1=bc[:, T:2 * T], op=OP.mult), inc=self.s_v)
            else:
                wl = [(self.s_sp1, spill_done[j - 1])] if j >= 1 else []
                b.op("vector", lambda e, j=j: e.tensor_tensor(
                    out=xm[:, :], in0=t1s[:, :], in1=bc[:, T:2 * T], op=OP.mult),
                    waits=wl, inc=self.s_v)
                sp_mark = self.s_v.n
                b.op("sync", lambda e, j=j: e.dma_start(
                    out=spill[j * 128:(j + 1) * 128, :], in_=xm[:, :]),
                    waits=[(self.s_v, sp_mark)], inc=self.s_sp1, k=16)
                spill_done[j] = self.s_sp1.n
                wl = [(self.s_s, aff_done[j - 1])] if j >= 1 else []
                b.op("vector", lambda e, j=j: e.tensor_tensor(
                    out=sq[:, 0:T], in0=t1s[:, :], in1=bc[:, 2 * T:3 * T],
                    op=OP.mult), waits=wl, inc=self.s_v)
                sc_mark = self.s_v.n
                if affine_spill is None:
                    b.op("scalar", lambda e, j=j: e.activation(
                        out=xn_sb[:, j * T:(j + 1) * T], in_=sq[:, 0:T],
                        func=AF.Identity,
                        scale=self.vsl(gcol, j, 1), bias=self.vsl(bcol, j, 1)),
                        waits=[(self.s_v, sc_mark)], inc=self.s_s)
                else:
                    wl2 = [(self.s_v, sc_mark)]
                    if j >= 2:
                        wl2.append((self.s_sp2, self._affsp[j - 2]))
                    b.op("scalar", lambda e, j=j: e.activation(
                        out=xn_sb[:, (j % 2) * T:(j % 2 + 1) * T], in_=sq[:, 0:T],
                        func=AF.Identity,
                        scale=self.vsl(gcol, j, 1), bias=self.vsl(bcol, j, 1)),
                        waits=wl2, inc=self.s_s)
                    mk2 = self.s_s.n
                    b.op("sync", lambda e, j=j: e.dma_start(
                        out=affine_spill[j * 128:(j + 1) * 128, :],
                        in_=xn_sb[:, (j % 2) * T:(j % 2 + 1) * T]),
                        waits=[(self.s_s, mk2)], inc=self.s_sp2, k=16)
                    if not hasattr(self, "_affsp"):
                        self._affsp = {}
                    self._affsp[j] = self.s_sp2.n
                aff_done[j] = self.s_s.n
            v_used[j] = self.s_v.n
        self.ln_xn_done_v = self.s_v.n
        self.ln_xn_done_s = self.s_s.n
        b.free(g6, g5, g4, g3, g2, g1)

    def emit_ffn0(self, xsrc, xdst):
        b = self.b
        xn, gxn = b.sbuf("f0xn", [128, NJ * T], BF16)
        self.emit_ln(xsrc, xn)
        b.barrier()
        ws1, gws1 = b.sbuf("f0ws1", [128, NJ * 4096], BF16)
        ws2, gws2 = b.sbuf("f0ws2", [128, 32 * 1024], BF16)
        al_sb, gal = b.sbuf("f0al", [128, 40], F32)
        hb2, ghb2 = b.sbuf("f0hb2", [128, 8], F32)
        psal, gpal = b.psum("f0pal", [128, 40])
        kinv1 = self.vsl("kinv1")
        kh2 = self.vsl("kh2")
        ones1 = self.vsl("ones", 0, 1)
        w1T, w2T = self.w["w1T"], self.w["w2T"]

        wtmp, gwt = b.sbuf("f0wt", [128, 2 * 4096], F32)
        alm = [0] * 32
        for j in range(NJ):
            wl = [] if j < 2 else [(self.s_t, alm[j - 2])]
            b.op("sync", lambda e, j=j: e.dma_start(
                out=wtmp[:, (j % 2) * 4096:(j % 2 + 1) * 4096],
                in_=w1T[j * 128:(j + 1) * 128, :]), waits=wl,
                inc=self.sw2[j % 2], k=16)
            ld = self.sw2[j % 2].n
            b.op("scalar", lambda e, j=j: e.activation(
                out=ws1[:, j * 4096:(j + 1) * 4096],
                in_=wtmp[:, (j % 2) * 4096:(j % 2 + 1) * 4096], func=AF.Sign),
                waits=[(self.sw2[j % 2], ld)], inc=self.s_s)
            b.op("scalar", lambda e, j=j: e.activation(
                out=wtmp[:, (j % 2) * 4096:(j % 2 + 1) * 4096],
                in_=wtmp[:, (j % 2) * 4096:(j % 2 + 1) * 4096], func=AF.Abs),
                inc=self.s_s)
            ab = self.s_s.n
            for fc in range(32):
                b.op("tensor", lambda e, j=j, fc=fc: e.matmul(
                    psal[:, fc:fc + 1],
                    wtmp[:, (j % 2) * 4096 + fc * 128:(j % 2) * 4096 + (fc + 1) * 128],
                    self._ones_col(),
                    start=(j == 0 and fc == 0), stop=False),
                    waits=[(self.s_s, ab)] if fc == 0 else [],
                    inc=self.s_t if fc == 31 else None)
            alm[j] = self.s_t.n
        sgn1_done = self.s_s.n
        w1_al_done = self.s_t.n
        b.free(gwt)

        wt2, gwt2 = b.sbuf("f0wt2", [128, 2 * 1024], F32)
        alm2 = [0] * 32
        for j2 in range(32):
            wl = [(self.s_t, alm[NJ - 1]), (self.s_s, sgn1_done)] if j2 < 2 \
                else [(self.s_t, alm2[j2 - 2])]
            b.op("sync", lambda e, j2=j2: e.dma_start(
                out=wt2[:, (j2 % 2) * 1024:(j2 % 2 + 1) * 1024],
                in_=w2T[j2 * 128:(j2 + 1) * 128, :]), waits=wl,
                inc=self.sw2[j2 % 2], k=16)
            ld = self.sw2[j2 % 2].n
            b.op("scalar", lambda e, j2=j2: e.activation(
                out=ws2[:, j2 * 1024:(j2 + 1) * 1024],
                in_=wt2[:, (j2 % 2) * 1024:(j2 % 2 + 1) * 1024], func=AF.Sign),
                waits=[(self.sw2[j2 % 2], ld)], inc=self.s_s)
            b.op("scalar", lambda e, j2=j2: e.activation(
                out=wt2[:, (j2 % 2) * 1024:(j2 % 2 + 1) * 1024],
                in_=wt2[:, (j2 % 2) * 1024:(j2 % 2 + 1) * 1024], func=AF.Abs),
                inc=self.s_s)
            ab = self.s_s.n
            for oc in range(8):
                b.op("tensor", lambda e, j2=j2, oc=oc: e.matmul(
                    psal[:, 32 + oc:33 + oc],
                    wt2[:, (j2 % 2) * 1024 + oc * 128:(j2 % 2) * 1024 + (oc + 1) * 128],
                    self._ones_col(),
                    start=False, stop=False),
                    waits=[(self.s_s, ab)] if oc == 0 else [],
                    inc=self.s_t if oc == 7 else None)
            alm2[j2] = self.s_t.n
        sgn2_done = self.s_s.n
        al_mm_done = self.s_t.n
        b.free(gwt2)
        b.op("vector", lambda e: e.tensor_scalar(
            out=al_sb[:, 0:32], in0=psal[:, 0:32], scalar1=1.0 / 1024.0,
            scalar2=None, op0=OP.mult), waits=[(self.s_t, al_mm_done)])
        b.op("vector", lambda e: e.tensor_scalar(
            out=al_sb[:, 32:40], in0=psal[:, 32:40], scalar1=0.5 / 4096.0,
            scalar2=None, op0=OP.mult))
        b.op("vector", lambda e: e.tensor_scalar(
            out=hb2[:, :], in0=self.vsl("b2"), scalar1=0.5, scalar2=None,
            op0=OP.mult), inc=self.s_v)
        al_done = self.s_v.n

        hbuf, ghb = b.sbuf("f0h", [128, 32 * TC], BF16)
        xre, gxr = b.sbuf("f0xr", [128, 2 * TC], F32)
        zt, gzt = b.sbuf("f0z", [128, 2 * TC], F32)
        psh = [b.psum(f"f0ph{i}", [128, TC]) for i in range(4)]
        psz = [b.psum(f"f0pz{i}", [128, TC]) for i in range(2)]
        hev = {}
        mm2m = {}
        zev = {}
        xadd = {}
        xldm = {}
        first = True
        for c in range(NTC):
            for fc in range(32):
                bank = fc % 4
                wl = []
                if first:
                    wl = [(self.s_s, sgn2_done), (self.s_v, self.ln_xn_done_v),
                          (self.s_s, self.ln_xn_done_s)]
                    first = False
                if c > 0 and fc < 4:
                    wl.append((self.s_s, hev[(c - 1, fc + 28)]))
                if fc >= 4:
                    wl.append((self.s_s, hev[(c, fc - 4)]))
                for j in range(NJ):
                    b.op("tensor", lambda e, j=j, fc=fc, c=c, bank=bank: e.matmul(
                        psh[bank][0][:, :],
                        ws1[:, j * 4096 + fc * 128:j * 4096 + (fc + 1) * 128],
                        xn[:, j * T + c * TC:j * T + (c + 1) * TC],
                        start=(j == 0), stop=(j == NJ - 1)),
                        waits=wl if j == 0 else [],
                        inc=self.s_t if j == NJ - 1 else None)
                mmark = self.s_t.n
                wl2 = [(self.s_t, mmark)]
                if c == 0 and fc == 0:
                    wl2.append((self.s_v, al_done))
                if c > 0:
                    wl2.append((self.s_t, mm2m[(c - 1, 7)]))
                b.op("scalar", lambda e, fc=fc, c=c, bank=bank: e.activation(
                    out=hbuf[:, fc * TC:(fc + 1) * TC],
                    in_=psh[bank][0][:, :], func=AF.Gelu,
                    scale=al_sb[:, fc:fc + 1], bias=self.vsl("b1", fc, 1)),
                    waits=wl2, inc=self.s_s)
                hev[(c, fc)] = self.s_s.n
            for oc in range(NJ):
                it = c * NJ + oc
                bank = oc % 2
                wl = []
                if oc == 0:
                    wl.append((self.s_s, hev[(c, 31)]))
                if it >= 2:
                    wl.append((self.s_s, zev[divmod(it - 2, NJ)]))
                wlx = []
                if it >= 2:
                    wlx.append((self.s_v, xadd[divmod(it - 2, NJ)]))
                else:
                    wlx.append((self.s_t, al_mm_done))
                b.op("sync", lambda e, c=c, oc=oc, it=it: e.dma_start(
                    out=xre[:, (it % 2) * TC:(it % 2 + 1) * TC],
                    in_=xsrc[oc * 128:(oc + 1) * 128, c * TC:(c + 1) * TC]),
                    waits=wlx, inc=self.sx2[it % 2], k=16)
                xldm[(c, oc)] = self.sx2[it % 2].n
                for j2 in range(32):
                    b.op("tensor", lambda e, j2=j2, oc=oc, c=c, bank=bank: e.matmul(
                        psz[bank][0][:, :],
                        ws2[:, j2 * 1024 + oc * 128:j2 * 1024 + (oc + 1) * 128],
                        hbuf[:, j2 * TC:(j2 + 1) * TC],
                        start=(j2 == 0), stop=(j2 == 31)),
                        waits=wl if j2 == 0 else [],
                        inc=self.s_t if j2 == 31 else None)
                mm2m[(c, oc)] = self.s_t.n
                b.op("scalar", lambda e, c=c, oc=oc, it=it, bank=bank: e.activation(
                    out=zt[:, (it % 2) * TC:(it % 2 + 1) * TC],
                    in_=psz[bank][0][:, :], func=AF.Identity,
                    scale=al_sb[:, 32 + oc:33 + oc], bias=hb2[:, oc:oc + 1]),
                    waits=[(self.s_t, mm2m[(c, oc)])] +
                          ([(self.s_v, xadd[divmod(it - 2, NJ)])] if it >= 2 else []),
                    inc=self.s_s)
                zev[(c, oc)] = self.s_s.n
                b.op("vector", lambda e, c=c, oc=oc, it=it: e.tensor_tensor(
                    out=xre[:, (it % 2) * TC:(it % 2 + 1) * TC],
                    in0=xre[:, (it % 2) * TC:(it % 2 + 1) * TC],
                    in1=zt[:, (it % 2) * TC:(it % 2 + 1) * TC], op=OP.add),
                    waits=[(self.s_s, zev[(c, oc)]),
                           (self.sx2[it % 2], xldm[(c, oc)])],
                    inc=self.s_v)
                xadd[(c, oc)] = self.s_v.n
                b.op("sync", lambda e, c=c, oc=oc, it=it: e.dma_start(
                    out=xdst[oc * 128:(oc + 1) * 128, c * TC:(c + 1) * TC],
                    in_=xre[:, (it % 2) * TC:(it % 2 + 1) * TC]),
                    waits=[(self.s_v, xadd[(c, oc)])], inc=self.s_st, k=16)
        for h, g in reversed(psh + psz):
            b.free(g)
        b.free(gpal)
        b.free(gzt, gxr, ghb)
        b.free(ghb2, gal, gws2, gws1, gxn)

    def _ones_col(self):
        return self.vsl("ones", 0, 1)

    def emit_ffn1(self, xsrc, xdst):
        b = self.b
        xn, gxn = b.sbuf("f1xn", [128, NJ * T], BF16)
        self.emit_ln(xsrc, xn, affine=("gam", "bet"), spill=self.xnspill)
        b.barrier()
        ws1, gws1 = b.sbuf("f1w1", [128, NJ * 4096], BF16)
        ws2, gws2 = b.sbuf("f1w2", [128, 32 * 1024], BF16)
        qb2, gqb2 = b.sbuf("f1qb2", [128, 8], F32)
        q1, q2 = self.w["q1"], self.w["q2"]
        sel8, sel32 = self.w["rowsel8"], self.w["rowsel32"]

        sc1sb, gs1 = b.sbuf("f1s1", [8, 4096], F32)
        s8sb, gs8 = b.sbuf("f1s8", [8, 8 * 128], F32)
        b.op("sync", lambda e: e.dma_start(out=sc1sb[:], in_=self.w["sc1"]),
             inc=self.s_ld, k=16)
        b.op("sync", lambda e: e.dma_start(out=s8sb[:], in_=sel8),
             inc=self.s_ld, k=16)
        sc_ld = self.s_ld.n
        psd = [b.psum(f"f1pd{i}", [128, TC]) for i in range(2)]

        qtmp, gqt = b.sbuf("f1qt", [128, 4096], F32)
        dq = {}
        for g_ in range(8):
            wl = [(self.s_ld, sc_ld)] if g_ == 0 else \
                 [(self.s_v, dq[(g_ - 1, 7)])]
            b.op("gpsimd", lambda e, g_=g_: e.dma_start(
                out=qtmp[:, :], in_=q1[g_ * 128:(g_ + 1) * 128, :]),
                waits=wl, inc=self.s_gld, k=16)
            ld = self.s_gld.n
            for ch in range(8):
                bank = ch % 2
                wl = [(self.s_v, dq[(g_, ch - 2)])] if ch >= 2 else (
                    [(self.s_v, dq[(g_ - 1, ch + 6)])] if g_ > 0 else
                    [(self.s_ld, sc_ld)])
                b.op("tensor", lambda e, g_=g_, ch=ch, bank=bank: e.matmul(
                    psd[bank][0][:, :], s8sb[:, g_ * 128:(g_ + 1) * 128],
                    sc1sb[:, ch * TC:(ch + 1) * TC], start=True, stop=True),
                    waits=wl + ([(self.s_gld, ld)] if ch == 0 else []),
                    inc=self.s_t)
                mk = self.s_t.n
                b.op("vector", lambda e, g_=g_, ch=ch, bank=bank: e.scalar_tensor_tensor(
                    out=ws1[:, g_ * 4096 + ch * TC:g_ * 4096 + (ch + 1) * TC],
                    in0=qtmp[:, ch * TC:(ch + 1) * TC], scalar=-128.0,
                    in1=psd[bank][0][:, :], op0=OP.add, op1=OP.mult),
                    waits=[(self.s_t, mk)], inc=self.s_v)
                dq[(g_, ch)] = self.s_v.n
        b.free(gqt, gs8, gs1)
        dq1_done = self.s_v.n

        sc2sb, gs2 = b.sbuf("f1s2", [32, 1024], F32)
        s32sb, gs32 = b.sbuf("f1s32", [32, 32 * 128], F32)
        b.op("sync", lambda e: e.dma_start(out=sc2sb[:], in_=self.w["sc2"]),
             waits=[(self.s_v, dq1_done), (self.s_t, self.s_t.n)],
             inc=self.s_ld, k=16)
        b.op("sync", lambda e: e.dma_start(out=s32sb[:], in_=sel32),
             inc=self.s_ld, k=16)
        sc_ld2 = self.s_ld.n
        qtmp2, gqt2 = b.sbuf("f1qt2", [128, 2 * 1024], F32)
        dq2 = {}
        for g_ in range(32):
            wl = [(self.s_ld, sc_ld2)] if g_ == 0 else []
            if g_ >= 2:
                wl.append((self.s_v, dq2[(g_ - 2, 1)]))
            b.op("gpsimd", lambda e, g_=g_: e.dma_start(
                out=qtmp2[:, (g_ % 2) * 1024:(g_ % 2 + 1) * 1024],
                in_=q2[g_ * 128:(g_ + 1) * 128, :]),
                waits=wl, inc=self.sw2[g_ % 2], k=16)
            ld = self.sw2[g_ % 2].n
            for ch in range(2):
                bank = ch % 2
                wl = []
                if g_ > 0:
                    wl.append((self.s_v, dq2[(g_ - 1, ch)]))
                else:
                    wl.append((self.s_ld, sc_ld2))
                b.op("tensor", lambda e, g_=g_, ch=ch, bank=bank: e.matmul(
                    psd[bank][0][:, :], s32sb[:, g_ * 128:(g_ + 1) * 128],
                    sc2sb[:, ch * TC:(ch + 1) * TC], start=True, stop=True),
                    waits=wl + ([(self.sw2[g_ % 2], ld)] if ch == 0 else []),
                    inc=self.s_t)
                mk = self.s_t.n
                b.op("vector", lambda e, g_=g_, ch=ch, bank=bank: e.scalar_tensor_tensor(
                    out=ws2[:, g_ * 1024 + ch * TC:g_ * 1024 + (ch + 1) * TC],
                    in0=qtmp2[:, (g_ % 2) * 1024 + ch * TC:(g_ % 2) * 1024 + (ch + 1) * TC],
                    scalar=-128.0, in1=psd[bank][0][:, :], op0=OP.add, op1=OP.mult),
                    waits=[(self.s_t, mk)], inc=self.s_v)
                dq2[(g_, ch)] = self.s_v.n
        b.free(gqt2, gs32, gs2)
        dq2_done = self.s_v.n
        b.op("vector", lambda e: e.tensor_scalar(
            out=qb2[:, :], in0=self.vsl("fc2b"), scalar1=0.25, scalar2=None,
            op0=OP.mult), inc=self.s_v)
        qb2_done = self.s_v.n

        hbuf, ghb = b.sbuf("f1h", [128, 32 * TC], BF16)
        xre, gxr = b.sbuf("f1xr", [128, 2 * TC], F32)
        zt, gzt = b.sbuf("f1z", [128, 2 * TC], F32)
        xnm, gxm = b.sbuf("f1xm", [128, 2 * TC], BF16)
        psh = [b.psum(f"f1ph{i}", [128, TC]) for i in range(4)]
        psz = [b.psum(f"f1pz{i}", [128, TC]) for i in range(2)]
        hev = {}
        mm2m = {}
        zev = {}
        xadd = {}
        xldm = {}
        first = True
        for c in range(NTC):
            for fc in range(32):
                bank = fc % 4
                wl = []
                if first:
                    wl = [(self.s_v, dq2_done), (self.s_v, self.ln_xn_done_v),
                          (self.s_s, self.ln_xn_done_s)]
                    first = False
                if c > 0 and fc < 4:
                    wl.append((self.s_s, hev[(c - 1, fc + 28)]))
                if fc >= 4:
                    wl.append((self.s_s, hev[(c, fc - 4)]))
                for j in range(NJ):
                    b.op("tensor", lambda e, j=j, fc=fc, c=c, bank=bank: e.matmul(
                        psh[bank][0][:, :],
                        ws1[:, j * 4096 + fc * 128:j * 4096 + (fc + 1) * 128],
                        xn[:, j * T + c * TC:j * T + (c + 1) * TC],
                        start=(j == 0), stop=(j == NJ - 1)),
                        waits=wl if j == 0 else [],
                        inc=self.s_t if j == NJ - 1 else None)
                mmark = self.s_t.n
                wl2 = [(self.s_t, mmark)]
                if c > 0:
                    wl2.append((self.s_t, mm2m[(c - 1, 7)]))
                b.op("scalar", lambda e, fc=fc, c=c, bank=bank: e.activation(
                    out=hbuf[:, fc * TC:(fc + 1) * TC],
                    in_=psh[bank][0][:, :], func=AF.Silu,
                    bias=self.vsl("fc1b", fc, 1)),
                    waits=wl2, inc=self.s_s)
                hev[(c, fc)] = self.s_s.n
            for oc in range(NJ):
                it = c * NJ + oc
                bank = oc % 2
                wl = []
                if oc == 0:
                    wl.append((self.s_s, hev[(c, 31)]))
                if it >= 2:
                    wl.append((self.s_s, zev[divmod(it - 2, NJ)]))
                wlx = []
                if it >= 2:
                    wlx.append((self.s_v, xadd[divmod(it - 2, NJ)]))
                else:
                    wlx.append((self.s_v, dq2_done))
                b.op("sync", lambda e, c=c, oc=oc, it=it: e.dma_start(
                    out=xre[:, (it % 2) * TC:(it % 2 + 1) * TC],
                    in_=xsrc[oc * 128:(oc + 1) * 128, c * TC:(c + 1) * TC]),
                    waits=wlx, inc=self.sx2[it % 2], k=16)
                b.op("sync", lambda e, c=c, oc=oc, it=it: e.dma_start(
                    out=xnm[:, (it % 2) * TC:(it % 2 + 1) * TC],
                    in_=self.xnspill[oc * 128:(oc + 1) * 128, c * TC:(c + 1) * TC]),
                    inc=self.sx2[it % 2], k=16)
                xldm[(c, oc)] = self.sx2[it % 2].n
                for j2 in range(32):
                    b.op("tensor", lambda e, j2=j2, oc=oc, c=c, bank=bank: e.matmul(
                        psz[bank][0][:, :],
                        ws2[:, j2 * 1024 + oc * 128:j2 * 1024 + (oc + 1) * 128],
                        hbuf[:, j2 * TC:(j2 + 1) * TC],
                        start=(j2 == 0), stop=(j2 == 31)),
                        waits=wl if j2 == 0 else [],
                        inc=self.s_t if j2 == 31 else None)
                mm2m[(c, oc)] = self.s_t.n
                b.op("scalar", lambda e, c=c, oc=oc, it=it, bank=bank: e.activation(
                    out=zt[:, (it % 2) * TC:(it % 2 + 1) * TC],
                    in_=psz[bank][0][:, :], func=AF.Identity,
                    scale=0.25, bias=qb2[:, oc:oc + 1]),
                    waits=[(self.s_t, mm2m[(c, oc)]), (self.s_v, qb2_done)] +
                          ([(self.s_v, xadd[divmod(it - 2, NJ)])] if it >= 2 else []),
                    inc=self.s_s)
                zev[(c, oc)] = self.s_s.n
                b.op("vector", lambda e, it=it: e.scalar_tensor_tensor(
                    out=xre[:, (it % 2) * TC:(it % 2 + 1) * TC],
                    in0=xnm[:, (it % 2) * TC:(it % 2 + 1) * TC], scalar=0.5,
                    in1=xre[:, (it % 2) * TC:(it % 2 + 1) * TC],
                    op0=OP.mult, op1=OP.add),
                    waits=[(self.s_s, zev[(c, oc)]),
                           (self.sx2[it % 2], xldm[(c, oc)])])
                b.op("vector", lambda e, it=it: e.tensor_tensor(
                    out=xre[:, (it % 2) * TC:(it % 2 + 1) * TC],
                    in0=xre[:, (it % 2) * TC:(it % 2 + 1) * TC],
                    in1=zt[:, (it % 2) * TC:(it % 2 + 1) * TC], op=OP.add),
                    inc=self.s_v)
                xadd[(c, oc)] = self.s_v.n
                b.op("sync", lambda e, c=c, oc=oc, it=it: e.dma_start(
                    out=xdst[oc * 128:(oc + 1) * 128, c * TC:(c + 1) * TC],
                    in_=xre[:, (it % 2) * TC:(it % 2 + 1) * TC]),
                    waits=[(self.s_v, xadd[(c, oc)])], inc=self.s_st, k=16)
        for h, g in reversed(psd + psh + psz):
            b.free(g)
        b.free(gxm, gzt, gxr, ghb)
        b.free(gqb2, gws2, gws1, gxn)

    def emit_attn(self, xsrc, xdst):
        b = self.b
        w = self.w
        xn, gxn = b.sbuf("atxn", [128, NJ * T], BF16)
        self.emit_ln(xsrc, xn)
        b.barrier()
        NH = 16
        eps1 = self.vec[0:1, VEC_COLS["eps"][0]:VEC_COLS["eps"][0] + 1][0:1, 0:1]
        hsel = self.vsl("hsel")

        qrot, gqr = b.sbuf("atQ", [128, 8 * T], BF16)
        krot, gkr = b.sbuf("atK", [128, 2 * T], BF16)
        vt, gvt = b.sbuf("atV", [128, 16 * 260], BF16)
        bnd, gbn = b.sbuf("atB", [16, T], BF16)
        ctxF, gcf = b.sbuf("atC", [128, 8 * T], BF16)
        idb, gid = b.sbuf("atI", [128, 128], BF16)
        cmk, gcm = b.sbuf("atM", [128, 4 * TC], BF16)
        obf, gob = b.sbuf("atO1", [1, 128], BF16)
        sml, gsm = b.sbuf("atS", [128, 16], F32)

        rq, grq = b.sbuf("atrq", [128, 2 * T], F32)
        rk, grk = b.sbuf("atrk", [128, 2 * T], F32)
        wqsb, gwq = b.sbuf("atwq", [128, 8 * 1024], BF16)
        wksb, gwk = b.sbuf("atwk", [128, 8 * 256], BF16)
        wvsb, gwv = b.sbuf("atwv", [128, 8 * 256], BF16)
        qraw, gqw = b.sbuf("atqr", [128, 8 * TC], F32)
        sqs, gsq = b.sbuf("atsq", [128, TC], F32)
        sqn, gsqn = b.sbuf("atsn", [128, TC], F32)
        n4, gn4 = b.sbuf("atn4", [4, 2 * TC], F32)
        bnd4, gb4 = b.sbuf("atb4", [4, TC], BF16)
        bvb, gbv = b.sbuf("atbv", [1, 256], BF16)
        psp = [b.psum(f"atpp{i}", [128, TC]) for i in range(2)]
        psn, gpn = b.psum("atpn", [4, TC])

        for nm, dst in [("rotq", rq), ("rotk", rk)]:
            b.op("sync", lambda e, nm=nm, dst=dst: e.dma_start(out=dst[:], in_=w[nm]),
                 inc=self.s_ld, k=16)
        b.op("sync", lambda e: e.dma_start(out=idb[:], in_=w["identb"]), inc=self.s_ld, k=16)
        b.op("sync", lambda e: e.dma_start(out=cmk[:], in_=w["cmask"]), inc=self.s_ld, k=16)
        b.op("sync", lambda e: e.dma_start(out=obf[:], in_=w["onesbf"]), inc=self.s_ld, k=16)
        for nm, dst, ncol in [("wq", wqsb, 1024), ("wk", wksb, 256), ("wv", wvsb, 256)]:
            for j in range(NJ):
                b.op("gpsimd", lambda e, nm=nm, dst=dst, ncol=ncol, j=j: e.dma_start(
                    out=dst[:, j * ncol:(j + 1) * ncol],
                    in_=w[nm][j * 128:(j + 1) * 128, :]), inc=self.s_gld, k=16)
        b.op("gpsimd", lambda e: e.dma_start(out=bvb[:], in_=w["bv"]), inc=self.s_gld, k=16)
        ld_t = self.s_ld.n
        gld_t = self.s_gld.n
        b.op("vector", lambda e: e.memset(vt[:, :], 1.0), inc=self.s_v)
        vt_ones = self.s_v.n

        def proj_block(dst_slot, wsb, wstride, oc, c, bias_name, bias_col, bk, wl0):
            cs = slice(c * TC, (c + 1) * TC)
            for j in range(NJ):
                b.op("tensor", lambda e, j=j: e.matmul(
                    psp[bk][0][:, :],
                    wsb[:, j * wstride + oc * 128:j * wstride + (oc + 1) * 128],
                    xn[:, j * T:(j + 1) * T][:, cs],
                    start=(j == 0), stop=(j == NJ - 1)),
                    waits=wl0 if j == 0 else [],
                    inc=self.s_t if j == NJ - 1 else None)
            mk = self.s_t.n
            b.op("scalar", lambda e: e.activation(
                out=qraw[:, dst_slot * TC:(dst_slot + 1) * TC], in_=psp[bk][0][:, :],
                func=AF.Identity, bias=self.vsl(bias_name, bias_col, 1)),
                waits=[(self.s_t, mk)], inc=self.s_s)
            return self.s_s.n

        def rot_pair(src_e, src_o, dst, tabs, c, wl0):
            cs = slice(c * TC, (c + 1) * TC)
            a_tb, b_tb, table = tabs
            b.op("vector", lambda e: e.tensor_tensor(
                out=sqs[:, :], in0=qraw[:, src_e * TC:(src_e + 1) * TC],
                in1=table[:, a_tb * T:(a_tb + 1) * T][:, cs], op=OP.mult), waits=wl0)
            b.op("vector", lambda e: e.tensor_tensor(
                out=rk[:, 0:TC] if table is rq else qraw[:, 2 * TC:3 * TC],
                in0=qraw[:, src_o * TC:(src_o + 1) * TC],
                in1=table[:, b_tb * T:(b_tb + 1) * T][:, cs], op=OP.mult))
            tmp = rk[:, 0:TC] if table is rq else qraw[:, 2 * TC:3 * TC]
            b.op("vector", lambda e, tmp=tmp: e.tensor_tensor(
                out=dst[:, cs], in0=sqs[:, :], in1=tmp, op=OP.subtract), inc=self.s_v)
            return self.s_v.n

        def norms(tile_e_ap, tile_o_ap, c, wl0):
            for o2, src in enumerate((tile_e_ap, tile_o_ap)):
                cs = slice(c * TC, (c + 1) * TC)
                b.op("scalar", lambda e, src=src, cs=cs: e.activation(
                    out=sqn[:, :], in_=src[:, cs], func=AF.Square),
                    waits=(wl0 if o2 == 0 else []) + [(self.s_t, self.s_t.n)],
                    inc=self.s_s)
                sqm = self.s_s.n
                b.op("tensor", lambda e, o2=o2: e.matmul(
                    psn[:, :], hsel, sqn[:, :], start=(o2 == 0), stop=(o2 == 1)),
                    waits=[(self.s_s, sqm)] +
                          ([(self.s_v, self.s_v.n)] if o2 == 0 else []),
                    inc=self.s_t if o2 == 1 else None)
            mk = self.s_t.n
            b.op("vector", lambda e: e.tensor_copy(out=n4[0:4, 0:TC], in_=psn[:, :]),
                 waits=[(self.s_t, mk)], inc=self.s_v)
            b.op("scalar", lambda e: e.activation(
                out=n4[0:4, TC:2 * TC], in_=n4[0:4, 0:TC], func=AF.Sqrt,
                bias=self.vsl("eps")[0:4, 0:1]),
                waits=[self.s_v.at()], inc=self.s_s)
            return self.s_s.n

        ev = {}
        for c in range(NTC):
            for o2 in range(2):
                idx = c * 2 + o2
                wl = []
                if idx == 0:
                    wl = [(self.s_gld, gld_t), (self.s_v, self.ln_xn_done_v)]
                if idx >= 2:
                    wl.append((self.s_s, ev[idx - 2]))
                ev[idx] = proj_block(o2, wksb, 256, o2, c, "bk", o2, o2, wl)
            wl = [(self.s_s, ev[c * 2 + 1]), (self.s_ld, ld_t)]
            rot_pair(0, 1, krot[:, 0:T], (0, 1, rk), c, wl)
            kro = rot_pair(0, 1, krot[:, T:2 * T], (1, 0, rk), c, [])
            nm = norms(krot[:, 0:T], krot[:, T:2 * T], c, [(self.s_v, kro)])
            b.op("vector", lambda e, c=c: e.reduce_max(
                out=sml[0:4, c:c + 1], in_=n4[0:4, TC:2 * TC], axis=AX.X),
                waits=[(self.s_s, nm)], inc=self.s_v)
        b.op("vector", lambda e: e.reduce_max(out=sml[0:4, 4:5], in_=sml[0:4, 0:4],
                                              axis=AX.X), inc=self.s_v)
        mkv = self.s_v.n
        b.op("tensor", lambda e: e.matmul(psn[0:1, 0:4], sml[0:4, 4:5],
                                          self.vsl("id4")[0:4, :], start=True,
                                          stop=True),
             waits=[(self.s_v, mkv)], inc=self.s_t)
        b.op("vector", lambda e: e.tensor_copy(out=sml[0:1, 8:12], in_=psn[0:1, 0:4]),
             waits=[self.s_t.at()], inc=self.s_v)
        b.op("vector", lambda e: e.reduce_max(out=sml[0:1, 5:6], in_=sml[0:1, 8:12],
                                              axis=AX.X))
        b.op("vector", lambda e: e.tensor_scalar(
            out=sml[0:1, 6:7], in0=sml[0:1, 5:6], scalar1=-1.01, scalar2=None,
            op0=OP.mult), inc=self.s_v)
        nmk = self.s_v.n
        b.op("tensor", lambda e: e.matmul(psn[:, 0:1], self.vec[0:1, 0:4],
                                          sml[0:1, 6:7], start=True, stop=True),
             waits=[(self.s_v, nmk)], inc=self.s_t)
        b.op("vector", lambda e: e.tensor_copy(out=sml[0:4, 7:8], in_=psn[:, 0:1]),
             waits=[self.s_t.at()], inc=self.s_v)
        negmk4 = self.s_v.n

        for c in range(NTC):
            qev = {}
            for oc in range(8):
                idx = c * 8 + oc
                wl = []
                if idx >= 2:
                    pass
                if oc >= 2:
                    wl.append((self.s_s, qev[oc - 2]))
                elif c > 0:
                    wl.append((self.s_s, self._qev_prev[oc]))
                qev[oc] = proj_block(oc, wqsb, 1024, oc, c, "bq", oc, oc % 2, wl)
            self._qev_prev = {0: qev[6], 1: qev[7]}
            for i in range(4):
                wl = [(self.s_s, qev[7]), (self.s_ld, ld_t)]
                rot_pair(i, 4 + i, qrot[:, i * T:(i + 1) * T], (0, 1, rq), c, wl)
                rd = rot_pair(i, 4 + i, qrot[:, (4 + i) * T:(5 + i) * T], (1, 0, rq), c, [])
                nm = norms(qrot[:, i * T:(i + 1) * T], qrot[:, (4 + i) * T:(5 + i) * T],
                           c, [(self.s_v, rd)])
                b.op("vector", lambda e: e.tensor_scalar(
                    out=bnd4[0:4, :], in0=n4[0:4, TC:2 * TC], scalar1=sml[0:4, 7:8],
                    scalar2=None, op0=OP.mult),
                    waits=[(self.s_s, nm), (self.s_v, negmk4),
                           (self.s_gld, self.s_gld.n)], inc=self.s_v)
                bm = self.s_v.n
                b.op("gpsimd", lambda e, i=i, c=c: e.dma_start(
                    out=bnd[4 * i:4 * i + 4, c * TC:(c + 1) * TC], in_=bnd4[0:4, :]),
                    waits=[(self.s_v, bm)], inc=self.s_gld, k=16)

        vev = {}
        for t16 in range(16):
            bk = t16 % 2
            wl = [(self.s_v, vt_ones), (self.s_gld, gld_t)]
            if t16 < 2:
                wl.append((self.s_s, self.s_s.n))
            if t16 >= 2:
                wl.append((self.s_v, vev[t16 - 2]))
            for j in range(NJ):
                b.op("tensor", lambda e, j=j, t16=t16, bk=bk: e.matmul(
                    psp[bk][0][:, 0:256],
                    xn[:, j * T + t16 * 128:j * T + (t16 + 1) * 128],
                    wvsb[:, j * 256:(j + 1) * 256], start=(j == 0), stop=False),
                    waits=wl if j == 0 else [])
            b.op("tensor", lambda e, bk=bk: e.matmul(
                psp[bk][0][:, 0:256], obf[0:1, :], bvb[0:1, :], start=False, stop=True),
                inc=self.s_t)
            mk = self.s_t.n
            for kv in range(4):
                b.op("vector", lambda e, t16=t16, kv=kv, bk=bk: e.tensor_copy(
                    out=vt[:, t16 * 260 + 65 * kv:t16 * 260 + 65 * kv + 64],
                    in_=psp[bk][0][:, 64 * kv:64 * kv + 64]),
                    waits=[(self.s_t, mk)] if kv == 0 else [],
                    inc=self.s_v if kv == 3 else None)
            vev[t16] = self.s_v.n
        b.free(gpn)
        for h_, g in reversed(psp):
            b.free(g)
        b.free(gbv, gb4, gn4, gsqn, gsq, gqw, gwv, gwk, gwq, grk, grq)

        k2, gk2 = b.sbuf("atk2", [128, 4 * T], BF16)
        q2h, gq2 = b.sbuf("atq2", [128, 3 * T], BF16)
        pts, gpt = b.sbuf("atpt", [128, 16 * TC], BF16)
        rrs, grr = b.sbuf("atrr", [1, TC], F32)
        rbs, grb = b.sbuf("atrb", [64, TC], F32)
        wosb, gwo = b.sbuf("atwo", [128, 8 * 1024], BF16)
        for j in range(NJ):
            b.op("gpsimd", lambda e, j=j: e.dma_start(
                out=wosb[:, j * 1024:(j + 1) * 1024],
                in_=w["wo"][j * 128:(j + 1) * 128, :]),
                waits=[(self.s_v, self.s_v.n)] if j == 0 else [],
                inc=self.s_gld, k=16)
        wo_ld = self.s_gld.n
        pss = [b.psum(f"atps{i}", [128, 2 * TC]) for i in range(2)]
        psc = [b.psum(f"atpc{i}", [65, TC]) for i in range(2)]
        psb, gpsb = b.psum("atpb", [64, TC])

        kro_all = self.s_v.n
        for kv in range(4):
            b.op("gpsimd", lambda e, kv=kv: e.dma_start(
                out=k2[0:32, kv * T:(kv + 1) * T],
                in_=krot[32 * kv:32 * kv + 32, 0:T]),
                waits=[(self.s_v, kro_all)], inc=self.s_gld, k=16)
            b.op("gpsimd", lambda e, kv=kv: e.dma_start(
                out=k2[32:64, kv * T:(kv + 1) * T],
                in_=krot[32 * kv:32 * kv + 32, T:2 * T]), inc=self.s_gld, k=16)
        b.op("vector", lambda e: e.memset(k2[64:65, :], 1.0), inc=self.s_v)
        k2_g = self.s_gld.n
        k2_v = self.s_v.n

        qasm = {}
        avm = {}
        nrm = {}
        exp_pair = {}
        for h_ in range(NH):
            kv = h_ // 4
            hh = h_ % 4
            slot = h_ % 3
            wl = []
            if h_ >= 3:
                wl.append((self.s_t, avm[(h_ - 3, NTC - 1)]))
            b.op("gpsimd", lambda e, slot=slot, hh=hh, kv=kv: e.dma_start(
                out=q2h[0:32, slot * T:(slot + 1) * T],
                in_=qrot[32 * hh:32 * hh + 32, kv * T:(kv + 1) * T]),
                waits=wl, inc=self.sq3[slot], k=16)
            b.op("gpsimd", lambda e, slot=slot, hh=hh, kv=kv: e.dma_start(
                out=q2h[32:64, slot * T:(slot + 1) * T],
                in_=qrot[32 * hh:32 * hh + 32, (4 + kv) * T:(5 + kv) * T]),
                inc=self.sq3[slot], k=16)
            b.op("gpsimd", lambda e, slot=slot, h_=h_: e.dma_start(
                out=q2h[64:65, slot * T:(slot + 1) * T], in_=bnd[h_:h_ + 1, :]),
                inc=self.sq3[slot], k=16)
            qasm[h_] = self.sq3[slot].n
            for c in range(NTC):
                cs = slice(c * TC, (c + 1) * TC)
                nkt = 4 * c + 4
                npr = nkt // 2
                for pr in range(npr):
                    bk = pr % 2
                    wl = []
                    if c == 0 and pr == 0:
                        wl += [(self.sq3[slot], qasm[h_]), (self.s_v, k2_v),
                               (self.s_gld, k2_g)]
                    if (h_, c, pr - 2) in exp_pair:
                        wl.append((self.s_s, exp_pair[(h_, c, pr - 2)]))
                    elif pr < 2 and (c > 0 or h_ > 0):
                        ph, pc2 = (h_, c - 1) if c > 0 else (h_ - 1, NTC - 1)
                        lp = (4 * pc2 + 4) // 2 - 1
                        key = (ph, pc2, max(0, lp - (1 - pr)))
                        wl.append((self.s_s, exp_pair[key]))
                    for half in range(2):
                        kt = 2 * pr + half
                        diag = kt >= 4 * c
                        hsl = slice(half * TC, (half + 1) * TC)
                        if diag:
                            r = kt - 4 * c
                            b.op("tensor", lambda e, bk=bk, hsl=hsl, r=r: e.matmul(
                                pss[bk][0][:, hsl], idb[:, :],
                                cmk[:, r * TC:(r + 1) * TC], start=True, stop=False),
                                waits=wl if half == 0 else [])
                            wl = []
                        b.op("tensor", lambda e, bk=bk, hsl=hsl, kt=kt, kv=kv,
                             slot=slot, cs=cs, diag=diag: e.matmul(
                            pss[bk][0][:, hsl],
                            k2[0:65, kv * T + kt * 128:kv * T + (kt + 1) * 128],
                            q2h[0:65, slot * T:(slot + 1) * T][:, cs],
                            start=(not diag), stop=True),
                            waits=wl if half == 0 else [],
                            inc=self.s_t if half == 1 else None)
                        wl = []
                    mk = self.s_t.n
                    wl2 = [(self.s_t, mk)]
                    if pr == 0 and (h_, c - 1) in avm:
                        wl2.append((self.s_t, avm[(h_, c - 1)]))
                    if pr == 0 and h_ > 0 and c == 0:
                        wl2.append((self.s_t, avm[(h_ - 1, NTC - 1)]))
                    b.op("scalar", lambda e, bk=bk, pr=pr: e.activation(
                        out=pts[:, pr * 2 * TC:(pr + 1) * 2 * TC],
                        in_=pss[bk][0][:, :], func=AF.Exp),
                        waits=wl2, inc=self.s_s)
                    exp_pair[(h_, c, pr)] = self.s_s.n
                cbk = c % 2
                wl = [(self.s_s, exp_pair[(h_, c, npr - 1)])]
                if (h_, c - 2) in nrm:
                    wl.append((self.s_v, nrm[(h_, c - 2)]))
                if c < 2 and h_ > 0:
                    wl.append((self.s_v, nrm[(h_ - 1, NTC - 2 + c)]))
                for kt in range(nkt):
                    b.op("tensor", lambda e, kt=kt, kv=kv, cbk=cbk: e.matmul(
                        psc[cbk][0][:, :],
                        vt[:, kt * 260 + 65 * kv:kt * 260 + 65 * kv + 65],
                        pts[:, kt * TC:(kt + 1) * TC], start=(kt == 0),
                        stop=(kt == nkt - 1)),
                        waits=wl if kt == 0 else [],
                        inc=self.s_t if kt == nkt - 1 else None)
                avm[(h_, c)] = self.s_t.n
                b.op("vector", lambda e, cbk=cbk: e.reciprocal(
                    out=rrs[0:1, :], in_=psc[cbk][0][64:65, :]),
                    waits=[(self.s_t, avm[(h_, c)])], inc=self.s_v)
                rm = self.s_v.n
                b.op("tensor", lambda e: e.matmul(
                    psb[:, :], self.vec[0:1, 0:64], rrs[0:1, :], start=True,
                    stop=True), waits=[(self.s_v, rm)], inc=self.s_t)
                bm2 = self.s_t.n
                b.op("vector", lambda e: e.tensor_copy(out=rbs[:, :], in_=psb[:, :]),
                     waits=[(self.s_t, bm2)], inc=self.s_v)
                b.op("vector", lambda e, h_=h_, cbk=cbk, cs=cs: e.tensor_tensor(
                    out=ctxF[64 * (h_ % 2):64 * (h_ % 2) + 64,
                             (h_ // 2) * T:(h_ // 2 + 1) * T][:, cs],
                    in0=psc[cbk][0][0:64, :], in1=rbs[:, :], op=OP.mult),
                    inc=self.s_v)
                nrm[(h_, c)] = self.s_v.n
        all_heads_v = self.s_v.n
        b.free(gpsb)
        for h_, g in reversed(psc):
            b.free(g)
        for h_, g in reversed(pss):
            b.free(g)

        xre, gxr = b.sbuf("atxre", [128, 2 * TC], F32)
        wo_ps = [b.psum(f"atwops{i}", [128, TC]) for i in range(2)]
        xadd = {}
        for oc in range(NJ):
            for c in range(NTC):
                it = oc * NTC + c
                bk = it % 2
                wl = [(self.s_v, all_heads_v), (self.s_gld, wo_ld)] if it == 0 else []
                if it >= 2:
                    wl.append((self.s_v, xadd[it - 2]))
                wlx = [(self.s_v, all_heads_v)] if it < 2 else                     [(self.s_v, xadd[it - 2])]
                b.op("sync", lambda e, oc=oc, c=c, it=it: e.dma_start(
                    out=xre[:, (it % 2) * TC:(it % 2 + 1) * TC],
                    in_=xsrc[oc * 128:(oc + 1) * 128, c * TC:(c + 1) * TC]),
                    waits=wlx, inc=self.sx2[it % 2], k=16)
                xl = self.sx2[it % 2].n
                for j in range(NJ):
                    b.op("tensor", lambda e, j=j, oc=oc, c=c, bk=bk: e.matmul(
                        wo_ps[bk][0][:, :],
                        wosb[:, j * 1024 + oc * 128:j * 1024 + (oc + 1) * 128],
                        ctxF[:, j * T + c * TC:j * T + (c + 1) * TC],
                        start=(j == 0), stop=(j == NJ - 1)),
                        waits=wl if j == 0 else [],
                        inc=self.s_t if j == NJ - 1 else None)
                mk = self.s_t.n
                b.op("vector", lambda e, oc=oc, it=it, bk=bk: e.scalar_tensor_tensor(
                    out=xre[:, (it % 2) * TC:(it % 2 + 1) * TC],
                    in0=wo_ps[bk][0][:, :], scalar=self.vsl("bo", oc, 1),
                    in1=xre[:, (it % 2) * TC:(it % 2 + 1) * TC],
                    op0=OP.add, op1=OP.add),
                    waits=[(self.s_t, mk), (self.sx2[it % 2], xl)], inc=self.s_v)
                xadd[it] = self.s_v.n
                b.op("sync", lambda e, oc=oc, c=c, it=it: e.dma_start(
                    out=xdst[oc * 128:(oc + 1) * 128, c * TC:(c + 1) * TC],
                    in_=xre[:, (it % 2) * TC:(it % 2 + 1) * TC]),
                    waits=[(self.s_v, xadd[it])], inc=self.s_st, k=16)
        for h_, g in reversed(wo_ps):
            b.free(g)
        b.free(gxr)
        b.free(gwo, grb, grr, gpt, gq2, gk2)
        b.free(gsm, gob, gcm, gid, gcf, gbn, gvt, gkr, gqr, gxn)

    def emit_mamba(self, xsrc, xdst):
        b = self.b
        w = self.w
        xnroll, gxr0 = b.sbuf("mmxnr", [128, 2 * T], BF16)
        self.emit_ln(xsrc, xnroll, affine=("mg", "mb"), spill=self.xnspill,
                     affine_spill=self.xnmsp)
        b.barrier()
        b.free(gxr0)
        NJ16 = 16

        xpw, gxpw = b.sbuf("mmxpw", [128, 16 * 96], F32)
        dpw, gdpw = b.sbuf("mmdpw", [64, 2048], BF16)
        apt, gapt = b.sbuf("mmapt", [128, 256], F32)
        rsc, grsc = b.sbuf("mmrsc", [128, 2], F32)
        rssb, grss = b.sbuf("mmrss", [1, 2], F32)
        xnmC, gxnm = b.sbuf("mmxnc", [128, 8 * TC], BF16)
        R1, gR1 = b.sbuf("mmR1", [128, 32768], BF16)
        R1f = R1.bitcast(F32)
        R3, gR3 = b.sbuf("mmR3", [128, 8240], F32)
        tails, gtl = b.sbuf("mmtl", [128, 48], F32)
        sz, gsz = b.sbuf("mmsz", [128, 16 * TC], BF16)
        xcf, gxcf = b.sbuf("mmxcf", [128, 16 * TC], F32)
        y8, gy8 = b.sbuf("mmy8", [128, 8 * TC], F32)
        bcb, gbcb = b.sbuf("mmbcb", [128, 2 * TC], F32)
        da, gda = b.sbuf("mmda", [128, TC], F32)
        dbx, gdbx = b.sbuf("mmdbx", [128, TC], F32)
        hh, ghh = b.sbuf("mmhh", [128, TC], F32)
        gg = dbx
        hst, ghst = b.sbuf("mmhst", [128, 256], F32)
        bcs, gbcs = b.sbuf("mmbcs", [32, TC], F32)
        xdbl, gxd = b.sbuf("mmxd", [64, TC], BF16)
        btmp, gbt = b.sbuf("mmbt", [1, TC], F32)
        xre, gxre = b.sbuf("mmxre", [128, TC], F32)
        xns2, gxns = b.sbuf("mmxns", [128, TC], BF16)
        pp = [b.psum(f"mmpp{i}", [128, TC]) for i in range(2)]
        p96, gp96 = b.psum("mmp96", [96, TC])
        pdt, gpdt = b.psum("mmpdt", [128, TC])
        pbc, gpbc = b.psum("mmpbc", [128, TC])
        pout = [b.psum(f"mmpo{i}", [128, TC]) for i in range(2)]
        onesr = self.vec[0:1, 0:128]

        def ipw_v(j):
            return R1[:, j * 4096:(j + 1) * 4096]

        def u_v(j):
            return R1f[:, j * TC:(j + 1) * TC]

        def opw_v(j):
            return R1[:, 16384 + j * 1024:16384 + (j + 1) * 1024]

        def xp_v(j):
            return R3[:, j * 515:j * 515 + 515]

        def dt_v(j):
            return R3[:, j * TC:(j + 1) * TC]

        b.op("sync", lambda e: e.dma_start(out=rssb[0:1, 0:1], in_=w["rs"]),
             inc=self.s_ld, k=16)
        rld = self.s_ld.n
        b.op("tensor", lambda e: e.matmul(pbc[:, 0:1], onesr, rssb[0:1, 0:1],
                                          start=True, stop=True),
             waits=[(self.s_ld, rld)], inc=self.s_t)
        b.op("vector", lambda e: e.tensor_copy(out=rsc[:, 0:1], in_=pbc[:, 0:1]),
             waits=[self.s_t.at()], inc=self.s_v)
        rs_done = self.s_v.n
        b.op("scalar", lambda e: e.activation(out=apt[:, :], in_=self.vsl("alog"),
                                              func=AF.Exp), inc=self.s_s)
        b.op("vector", lambda e: e.tensor_scalar(
            out=apt[:, :], in0=apt[:, :], scalar1=-1.0, scalar2=None, op0=OP.mult),
            waits=[self.s_s.at()], inc=self.s_v)
        apt_done = self.s_v.n
        b.op("gpsimd", lambda e: e.dma_start(out=dpw[:, :], in_=w["dpw"]),
             inc=self.sq3[0], k=16)
        for j in range(NJ16):
            b.op("sync", lambda e, j=j: e.dma_start(
                out=xpw[:, j * 96:(j + 1) * 96], in_=w["xpw"][j * 128:(j + 1) * 128, :]),
                inc=self.s_ld, k=16)
        b.op("vector", lambda e: e.memset(tails[:, :], 0.0), inc=self.s_v)
        setup_v = self.s_v.n
        setup_g = self.sq3[0].n
        setup_l = self.s_ld.n

        prev_c1mm = 0
        prev_u_used = 0
        prev_opw_used = 0
        prev_tail = 0
        prev_sz_used = 0
        prev_hsave = {}
        ipw_ld = 0
        opw_ld = 0
        prev_scan = 0
        prev_xcf_used = 0

        for c in range(NTC):
            csl = slice(c * TC, (c + 1) * TC)
            wlw = [(self.s_v, prev_u_used), (self.s_t, prev_opw_used)]
            if c == 0:
                for j in range(NJ):
                    b.op("gpsimd", lambda e, j=j: e.dma_start(
                        out=ipw_v(j), in_=w["ipw"][j * 128:(j + 1) * 128, :]),
                        waits=wlw if j == 0 else [], inc=self.s_gld, k=16)
                    wlw = []
                ipw_ld = self.s_gld.n
                for j in range(NJ):
                    b.op("gpsimd", lambda e, j=j: e.dma_start(
                        out=self.ipwc[j * 128:(j + 1) * 128, :], in_=ipw_v(j)),
                        inc=self.sq3[2], k=16)
                ipwc_st = self.sq3[2].n
            else:
                for j in range(NJ):
                    b.op("sync", lambda e, j=j: e.dma_start(
                        out=ipw_v(j), in_=self.ipwc[j * 128:(j + 1) * 128, :]),
                        waits=(wlw + [(self.sq3[2], ipwc_st)]) if j == 0 else [],
                        inc=self.s_ld, k=16)
                    wlw = []
                ipw_ld = self.s_ld.n
            for j in range(NJ):
                b.op("sync", lambda e, j=j, c=c: e.dma_start(
                    out=xnmC[:, j * TC:(j + 1) * TC],
                    in_=self.xnmsp[j * 128:(j + 1) * 128, c * TC:(c + 1) * TC]),
                    waits=[(self.s_t, prev_c1mm), (self.s_sp2, self._affsp[NJ - 1])]
                    if j == 0 else [], inc=self.s_ld, k=16)
            xnm_ld = self.s_ld.n

            ev = {}
            for ec in range(32):
                bank = ec % 2
                wl = []
                if ec == 0:
                    wl = [(self.s_ld, xnm_ld), (self.s_v, setup_v)]
                    if c == 0:
                        wl.append((self.s_gld, ipw_ld))
                    else:
                        wl.append((self.s_ld, ipw_ld))
                if ec >= 2:
                    wl.append((self.s_s, ev[ec - 2]))
                if ec < 2 and c > 0:
                    wl.append((self.s_s, self._mm_ev_prev[ec]))
                for j in range(NJ):
                    b.op("tensor", lambda e, j=j, ec=ec, bank=bank: e.matmul(
                        pp[bank][0][:, :], ipw_v(j)[:, ec * 128:(ec + 1) * 128],
                        xnmC[:, j * TC:(j + 1) * TC],
                        start=(j == 0), stop=(j == NJ - 1)),
                        waits=wl if j == 0 else [],
                        inc=self.s_t if j == NJ - 1 else None)
                mk = self.s_t.n
                if ec < 16:
                    wl2 = [(self.s_t, mk)]
                    if c > 0 and ec < 2:
                        wl2.append((self.s_v, prev_tail))
                    b.op("scalar", lambda e, ec=ec, bank=bank: e.activation(
                        out=xp_v(ec)[:, 3:3 + TC], in_=pp[bank][0][:, :],
                        func=AF.Identity), waits=wl2, inc=self.s_s)
                else:
                    wl2 = [(self.s_t, mk)]
                    if ec < 18 and c > 0:
                        wl2.append((self.s_t, prev_sz_used))
                    b.op("scalar", lambda e, ec=ec, bank=bank: e.activation(
                        out=sz[:, (ec - 16) * TC:(ec - 15) * TC],
                        in_=pp[bank][0][:, :], func=AF.Silu), waits=wl2, inc=self.s_s)
                ev[ec] = self.s_s.n
            prev_c1mm = self.s_t.n
            self._mm_ev_prev = {0: ev[30], 1: ev[31]}

            cv = {}
            for j in range(NJ16):
                wl = [(self.s_s, ev[j])]
                if c > 0:
                    pass
                b.op("vector", lambda e, j=j: e.tensor_copy(
                    out=xp_v(j)[:, 0:3], in_=tails[:, j * 3:(j + 1) * 3]), waits=wl)
                b.op("vector", lambda e, j=j: e.tensor_scalar(
                    out=hh[:, :], in0=xp_v(j)[:, 0:TC],
                    scalar1=self.vsl("cw", 4 * j, 1), scalar2=self.vsl("convb", j, 1),
                    op0=OP.mult, op1=OP.add),
                    waits=[(self.s_s, cv[j - 1])] if j >= 1 else [])
                for kq in range(1, 4):
                    b.op("vector", lambda e, j=j, kq=kq: e.scalar_tensor_tensor(
                        out=hh[:, :], in0=xp_v(j)[:, kq:kq + TC],
                        scalar=self.vsl("cw", 4 * j + kq, 1), in1=hh[:, :],
                        op0=OP.mult, op1=OP.add))
                b.op("vector", lambda e, j=j: e.tensor_copy(
                    out=tails[:, j * 3:(j + 1) * 3], in_=xp_v(j)[:, TC:TC + 3]),
                    inc=self.s_v)
                tm = self.s_v.n
                b.op("scalar", lambda e, j=j: e.activation(
                    out=xcf[:, j * TC:(j + 1) * TC], in_=hh[:, :], func=AF.Silu),
                    waits=[(self.s_v, tm), (self.s_v, prev_xcf_used)
                           if (c > 0 and j < 1) else (self.s_v, 0)], inc=self.s_s)
                cv[j] = self.s_s.n
            prev_tail = self.s_v.n

            for j in range(NJ16):
                b.op("tensor", lambda e, j=j: e.matmul(
                    p96[:, :], xpw[:, j * 96:(j + 1) * 96],
                    xcf[:, j * TC:(j + 1) * TC], start=(j == 0), stop=(j == NJ16 - 1)),
                    waits=([(self.s_s, cv[j])] if j < NJ16 else []) +
                          ([(self.s_ld, setup_l), (self.s_v, self._p96ev)]
                           if (j == 0 and c > 0) else
                           ([(self.s_ld, setup_l)] if j == 0 else [])),
                    inc=self.s_t if j == NJ16 - 1 else None)
            mk96 = self.s_t.n
            b.op("vector", lambda e: e.tensor_copy(out=bcs[:, :], in_=p96[64:96, :]),
                 waits=[(self.s_t, mk96)])
            b.op("vector", lambda e: e.tensor_copy(out=xdbl[:, :], in_=p96[0:64, :]),
                 inc=self.s_v)
            self._p96ev = self.s_v.n
            bc_done = self.s_v.n

            dtm = {}
            for j in range(NJ16):
                wl = [(self.s_v, bc_done), (self.sq3[0], setup_g)]
                if j > 0:
                    wl = [(self.s_s, dtm[j - 1])]
                b.op("tensor", lambda e, j=j: e.matmul(
                    pdt[:, :], dpw[0:64, j * 128:(j + 1) * 128], xdbl[:, :],
                    start=True, stop=True), waits=wl, inc=self.s_t)
                mk = self.s_t.n
                wl_abs = [(self.s_t, mk)]
                if j > 0:
                    wl_abs.append((self.s_v, self._dtadd))
                b.op("scalar", lambda e, j=j: e.activation(
                    out=da[:, :], in_=pdt[:, :], func=AF.Abs,
                    bias=self.vsl("dtb", j, 1)), waits=wl_abs)
                b.op("scalar", lambda e, j=j: e.activation(
                    out=dt_v(j)[:, :], in_=pdt[:, :], func=AF.Relu,
                    bias=self.vsl("dtb", j, 1)), inc=self.s_s)
                dtm[j] = self.s_s.n
                b.op("scalar", lambda e: e.activation(
                    out=da[:, :], in_=da[:, :], func=AF.Exp, scale=-1.0))
                b.op("scalar", lambda e: e.activation(
                    out=da[:, :], in_=da[:, :], func=AF.Ln,
                    bias=self.vsl("ones", 0, 1)), inc=self.s_s)
                lnm = self.s_s.n
                b.op("vector", lambda e, j=j: e.tensor_tensor(
                    out=dt_v(j)[:, :], in0=dt_v(j)[:, :], in1=da[:, :], op=OP.add),
                    waits=[(self.s_s, lnm)], inc=self.s_v)
                self._dtadd = self.s_v.n
                b.op("vector", lambda e, j=j: e.tensor_tensor(
                    out=u_v(j), in0=dt_v(j)[:, :], in1=xcf[:, j * TC:(j + 1) * TC],
                    op=OP.mult), inc=self.s_v)
            u_done = self.s_v.n

            wlw = [(self.s_v, u_done)]
            if c == 0:
                for j in range(NJ16):
                    b.op("gpsimd", lambda e, j=j: e.dma_start(
                        out=opw_v(j), in_=w["opw"][j * 128:(j + 1) * 128, :]),
                        waits=wlw if j == 0 else [], inc=self.s_gld, k=16)
                    wlw = []
                opw_ld = self.s_gld.n
                for j in range(NJ16):
                    b.op("gpsimd", lambda e, j=j: e.dma_start(
                        out=self.opwc[j * 128:(j + 1) * 128, :], in_=opw_v(j)),
                        inc=self.s_sp1, k=16)
                opwc_st = self.s_sp1.n
            else:
                for j in range(NJ16):
                    b.op("sync", lambda e, j=j: e.dma_start(
                        out=opw_v(j), in_=self.opwc[j * 128:(j + 1) * 128, :]),
                        waits=(wlw + [(self.s_sp1, opwc_st),
                                      (self.s_t, prev_opw_used)]) if j == 0 else [],
                        inc=self.s_ld, k=16)
                    wlw = []
                opw_ld = self.s_ld.n

            gate = {}
            for jh in range(2):
                for s_ in range(16):
                    wl = [(self.s_v, bc_done)]
                    b.op("gpsimd", lambda e, s_=s_: e.dma_start(
                        out=btmp[0:1, :], in_=bcs[s_:s_ + 1, :]),
                        waits=wl + [(self.s_t, self.s_t.n)], inc=self.sq3[1], k=16)
                    bt1 = self.sq3[1].n
                    b.op("tensor", lambda e: e.matmul(
                        pbc[:, :], onesr, btmp[0:1, :], start=True, stop=True),
                        waits=[(self.sq3[1], bt1), (self.s_v, self.s_v.n)],
                        inc=self.s_t)
                    b.op("vector", lambda e: e.tensor_copy(out=bcb[:, 0:TC],
                                                           in_=pbc[:, :]),
                         waits=[self.s_t.at()], inc=self.s_v)
                    bm1 = self.s_v.n
                    b.op("gpsimd", lambda e, s_=s_: e.dma_start(
                        out=btmp[0:1, :], in_=bcs[16 + s_:17 + s_, :]),
                        waits=[(self.s_t, self.s_t.n)], inc=self.sq3[1], k=16)
                    bt2 = self.sq3[1].n
                    b.op("tensor", lambda e: e.matmul(
                        pbc[:, :], onesr, btmp[0:1, :], start=True, stop=True),
                        waits=[(self.sq3[1], bt2), (self.s_v, bm1)], inc=self.s_t)
                    b.op("vector", lambda e: e.tensor_copy(out=bcb[:, TC:2 * TC],
                                                           in_=pbc[:, :]),
                         waits=[self.s_t.at()], inc=self.s_v)
                    for jj in range(8):
                        j = jh * 8 + jj
                        blk = j * 16 + s_
                        wl = [(self.s_v, prev_scan), (self.s_v, apt_done)]
                        if jh == 0 and s_ == 0 and jj == 0:
                            wl.append((self.s_v, u_done))
                        b.op("scalar", lambda e, j=j, blk=blk: e.activation(
                            out=da[:, :], in_=dt_v(j)[:, :], func=AF.Exp,
                            scale=apt[:, blk:blk + 1]), waits=wl, inc=self.s_s)
                        da_m = self.s_s.n
                        b.op("vector", lambda e, j=j: e.tensor_tensor(
                            out=dbx[:, :], in0=u_v(j), in1=bcb[:, 0:TC], op=OP.mult))
                        if c == 0:
                            b.op("vector", lambda e: e.tensor_tensor_scan(
                                out=hh[:, :], data0=da[:, :], data1=dbx[:, :],
                                initial=0.0, op0=OP.mult, op1=OP.add),
                                waits=[(self.s_s, da_m)])
                        else:
                            b.op("vector", lambda e, blk=blk: e.tensor_tensor_scan(
                                out=hh[:, :], data0=da[:, :], data1=dbx[:, :],
                                initial=hst[:, blk:blk + 1], op0=OP.mult, op1=OP.add),
                                waits=[(self.s_s, da_m)])
                        b.op("vector", lambda e, blk=blk: e.tensor_copy(
                            out=hst[:, blk:blk + 1], in_=hh[:, TC - 1:TC]),
                            inc=self.s_v)
                        prev_scan = self.s_v.n
                        b.op("vector", lambda e: e.tensor_tensor(
                            out=gg[:, :], in0=hh[:, :], in1=bcb[:, TC:2 * TC],
                            op=OP.mult))
                        if s_ == 0:
                            b.op("vector", lambda e, jj=jj: e.tensor_copy(
                                out=y8[:, jj * TC:(jj + 1) * TC], in_=gg[:, :]),
                                inc=self.s_v)
                        else:
                            b.op("vector", lambda e, jj=jj: e.tensor_tensor(
                                out=y8[:, jj * TC:(jj + 1) * TC],
                                in0=y8[:, jj * TC:(jj + 1) * TC], in1=gg[:, :],
                                op=OP.add), inc=self.s_v)
                for jj in range(8):
                    j = jh * 8 + jj
                    b.op("vector", lambda e, j=j, jj=jj: e.scalar_tensor_tensor(
                        out=y8[:, jj * TC:(jj + 1) * TC],
                        in0=xcf[:, j * TC:(j + 1) * TC], scalar=self.vsl("dd", j, 1),
                        in1=y8[:, jj * TC:(jj + 1) * TC], op0=OP.mult, op1=OP.add))
                    b.op("vector", lambda e, j=j, jj=jj: e.tensor_tensor(
                        out=sz[:, j * TC:(j + 1) * TC],
                        in0=y8[:, jj * TC:(jj + 1) * TC],
                        in1=sz[:, j * TC:(j + 1) * TC], op=OP.mult), inc=self.s_v)
                    gate[j] = self.s_v.n
            prev_xcf_used = self.s_v.n

            oadd = {}
            stm = {}
            for oc in range(NJ):
                bank = oc % 2
                wl = [(self.s_v, gate[15])]
                if c == 0:
                    wl.append((self.s_gld, opw_ld))
                else:
                    wl.append((self.s_ld, opw_ld))
                if oc >= 2:
                    wl.append((self.s_v, oadd[oc - 2]))
                wlx = [(self.s_st, stm[oc - 1])] if oc >= 1 else [(self.s_v, gate[15])]
                b.op("sync", lambda e, oc=oc, c=c: e.dma_start(
                    out=xre[:, :],
                    in_=xsrc[oc * 128:(oc + 1) * 128, c * TC:(c + 1) * TC]),
                    waits=wlx, inc=self.s_x, k=16)
                b.op("sync", lambda e, oc=oc, c=c: e.dma_start(
                    out=xns2[:, :],
                    in_=self.xnspill[oc * 128:(oc + 1) * 128, c * TC:(c + 1) * TC]),
                    inc=self.s_x, k=16)
                xl = self.s_x.n
                for j in range(NJ16):
                    b.op("tensor", lambda e, j=j, oc=oc, bank=bank: e.matmul(
                        pout[bank][0][:, :],
                        opw_v(j)[:, oc * 128:(oc + 1) * 128],
                        sz[:, j * TC:(j + 1) * TC],
                        start=(j == 0), stop=(j == NJ16 - 1)),
                        waits=wl if j == 0 else [],
                        inc=self.s_t if j == NJ16 - 1 else None)
                mk = self.s_t.n
                b.op("vector", lambda e, oc=oc, bank=bank: e.scalar_tensor_tensor(
                    out=xre[:, :],
                    in0=pout[bank][0][:, :], scalar=rsc[:, 0:1],
                    in1=xre[:, :],
                    op0=OP.mult, op1=OP.add),
                    waits=[(self.s_t, mk), (self.s_x, xl), (self.s_v, rs_done)])
                b.op("vector", lambda e, oc=oc: e.tensor_tensor(
                    out=xre[:, :],
                    in0=xre[:, :],
                    in1=xns2[:, :], op=OP.add),
                    inc=self.s_v)
                oadd[oc] = self.s_v.n
                b.op("sync", lambda e, oc=oc, c=c: e.dma_start(
                    out=xdst[oc * 128:(oc + 1) * 128, c * TC:(c + 1) * TC],
                    in_=xre[:, :]),
                    waits=[(self.s_v, oadd[oc])], inc=self.s_st, k=16)
                stm[oc] = self.s_st.n
            prev_opw_used = self.s_t.n
            prev_u_used = self.s_v.n
            prev_sz_used = self.s_t.n

        for h_, g in reversed(pout):
            b.free(g)
        b.free(gpbc, gpdt, gp96)
        for h_, g in reversed(pp):
            b.free(g)
        b.free(gxns, gxre, gbt, gxd, gbcs, ghst, ghh, gdbx, gda, gbcb,
               gy8, gxcf, gsz, gtl, gR3, gR1, gxnm, grss, grsc, gapt, gdpw,
               gxpw)


def host_prep(inp):
    sh = {}
    sh["vec"] = build_vec_blob(inp)
    sh["w1T"] = np.ascontiguousarray(np.asarray(inp["L0_ffn_w1"]).T)
    sh["w2T"] = np.ascontiguousarray(np.asarray(inp["L0_ffn_w2"]).T)
    sh["wq"] = perm_q(np.asarray(inp["L0_attn_Wq"]))
    sh["wk"] = perm_k(np.asarray(inp["L0_attn_Wk"]))
    sh["wv"] = np.ascontiguousarray(np.asarray(inp["L0_attn_Wv"]))
    sh["bv"] = np.asarray(inp["L0_attn_bv"]).reshape(1, 256)
    sh["wo"] = np.ascontiguousarray(np.asarray(inp["L0_attn_Wo"]))
    j = np.arange(32, dtype=np.float64)
    inv = 10000.0 ** (-2.0 * j / 64.0)
    tpos = np.arange(T, dtype=np.float64)
    fr = tpos[None, :] * inv[:, None]
    fr = np.tile(fr, (4, 1))
    cos, sin = np.cos(fr), np.sin(fr)
    sh["rotq"] = np.concatenate([cos / 8.0, sin / 8.0], axis=1).astype(np.float32)
    sh["rotk"] = np.concatenate([cos, sin], axis=1).astype(np.float32)
    masks = []
    kk = np.arange(128)[:, None]
    qq = np.arange(TC)[None, :]
    for r in range(4):
        masks.append(np.where(kk <= qq - 128 * r, 0.0, -30000.0))
    sh["cmask"] = np.concatenate(masks, axis=1).astype(BF16NP)
    sh["identb"] = np.eye(128, dtype=BF16NP)
    sh["onesbf"] = np.ones((1, 128), dtype=BF16NP)
    sh["ipw"] = np.ascontiguousarray(np.asarray(inp["L1_attn_in_proj_w"]).T)
    sh["xpw"] = np.ascontiguousarray(np.asarray(inp["L1_attn_x_proj_w"]).T)
    sh["dpw"] = np.ascontiguousarray(np.asarray(inp["L1_attn_dt_proj_w"]).T)
    sh["opw"] = np.ascontiguousarray(np.asarray(inp["L1_attn_out_proj_w"]).T)
    sh["rs"] = np.asarray(inp["L1_attn_res_scale"]).reshape(1, 1).astype(np.float32)
    sh["q1"] = np.ascontiguousarray(np.asarray(inp["L1_ffn_fc1_q"], dtype=np.int32))
    sh["q2"] = np.ascontiguousarray(np.asarray(inp["L1_ffn_fc2_q"], dtype=np.int32))
    sh["sc1"] = np.ascontiguousarray(np.asarray(inp["L1_ffn_fc1_scales"]))
    sh["sc2"] = np.ascontiguousarray(np.asarray(inp["L1_ffn_fc2_scales"]))
    rs8 = np.zeros((8, 8 * 128), np.float32)
    for g in range(8):
        rs8[g, g * 128:(g + 1) * 128] = 1.0
    sh["rowsel8"] = rs8
    rs32 = np.zeros((32, 32 * 128), np.float32)
    for g in range(32):
        rs32[g, g * 128:(g + 1) * 128] = 1.0
    sh["rowsel32"] = rs32
    return sh


_PROG_CACHE = {}


def get_prog(stages):
    key = tuple(stages)
    if key not in _PROG_CACHE:
        k = Kern(list(stages))
        k.emit()
        _PROG_CACHE[key] = k
    return _PROG_CACHE[key]


def kernel(**inp):
    sh = host_prep(inp)
    x = np.asarray(inp["L0_attn_x"])
    xcore = [np.ascontiguousarray(x[b].T) for b in range(8)]
    used_keys = None
    for stages in STAGE_SETS:
        k = get_prog(stages)
        names = set()
        for nm in list(k.w.keys()) + ["vec"]:
            names.add(nm)
        in_maps = []
        for b_ in range(8):
            m = {nm: sh[nm] for nm in names}
            m["xin"] = xcore[b_]
            in_maps.append(m)
        res = run_bass_kernel_spmd(k.nc, in_maps, core_ids=list(range(8)))
        xcore = [np.ascontiguousarray(res.results[b_]["xout"]) for b_ in range(8)]
    out = np.stack([xc.T for xc in xcore]).astype(np.float32)
    return np.ascontiguousarray(out)


# version 09
# speedup vs baseline: 1.1997x; 1.1997x over previous
import numpy as np
import ml_dtypes
import concourse.bass as bass
import concourse.mybir as mybir
from concourse.alu_op_type import AluOpType as OP
from concourse.bass_utils import run_bass_kernel_spmd

F32 = mybir.dt.float32
BF16 = mybir.dt.bfloat16
I32 = mybir.dt.int32
AF = mybir.ActivationFunctionType
AX = mybir.AxisListType
BF16NP = ml_dtypes.bfloat16

T = 2048
D = 1024
NJ = D // 128
TC = 512
NTC = T // TC

ALL_STAGES = ["ffn0a", "attn0", "ffn0b", "ffn1a", "mamba1", "ffn1b"]
STAGE_SETS = [ALL_STAGES]


class Sem:
    def __init__(self, handle):
        self.h = handle
        self.n = 0

    def bump(self, k=1):
        self.n += k
        return self.n

    def at(self):
        return (self, self.n)


class B:
    ENGS = ("sync", "scalar", "vector", "tensor", "gpsimd")

    def __init__(self, nc):
        self.nc = nc
        self.streams = {e: [] for e in self.ENGS}
        self._guards = []
        self.bar = None
        self._uid = 0

    def sem(self, name):
        g = self.nc.semaphore(name)
        h = g.__enter__()
        self._guards.append(g)
        return Sem(h)

    def sbuf(self, name, shape, dtype):
        self._uid += 1
        g = self.nc.sbuf_tensor(f"{name}_{self._uid}", shape, dtype)
        return g.__enter__(), g

    def psum(self, name, shape, dtype=F32):
        self._uid += 1
        g = self.nc.psum_tensor(f"{name}_{self._uid}", shape, dtype)
        return g.__enter__(), g

    def free(self, *guards):
        for g in guards:
            g.__exit__(None, None, None)

    def op(self, eng, fn, waits=(), inc=None, k=1):
        ws = [(s.h, n) for (s, n) in waits]
        ih = ik = None
        if inc is not None:
            inc.bump(k)
            ih, ik = inc.h, k

        def run(e, ws=ws, fn=fn, ih=ih, ik=ik):
            for h, n in ws:
                e.wait_ge(h, n)
            ins = fn(e)
            if ih is not None:
                ins.then_inc(ih, ik)
        self.streams[eng].append(run)

    def wait(self, eng, *waits):
        ws = [(s.h, n) for (s, n) in waits]

        def run(e, ws=ws):
            for h, n in ws:
                e.wait_ge(h, n)
        self.streams[eng].append(run)

    def barrier(self):
        if self.bar is None:
            self.bar = self.sem("bar")
        self.bar.n += 5
        target = self.bar.n

        def mk(e, target=target):
            e.sem_inc(self.bar.h, 1)
            e.wait_ge(self.bar.h, target)
        for eng in self.ENGS:
            self.streams[eng].append(mk)

    def finalize(self):
        nc = self.nc
        with nc.Block() as block:
            @block.sync
            def _(e):
                for fn in self.streams["sync"]:
                    fn(e)

            @block.scalar
            def _(e):
                for fn in self.streams["scalar"]:
                    fn(e)

            @block.vector
            def _(e):
                for fn in self.streams["vector"]:
                    fn(e)

            @block.tensor
            def _(e):
                for fn in self.streams["tensor"]:
                    fn(e)

            @block.gpsimd
            def _(e):
                for fn in self.streams["gpsimd"]:
                    fn(e)
        for g in reversed(self._guards):
            g.__exit__(None, None, None)


_VEC_ORDER = [
    ("ones", 128),
    ("eps", 1),
    ("kinv1", 1),
    ("kh2", 1),
    ("b1", 32), ("b2", 8),
    ("bq", 8), ("bk", 2), ("bo", 8),
    ("gam", 8), ("bet", 8),
    ("fc1b", 32), ("fc2b", 8),
    ("cw", 64), ("convb", 16), ("dtb", 16), ("dd", 16),
    ("mg", 8), ("mb", 8),
    ("alog", 256),
    ("hsel", 4), ("id4", 4),
]
VEC_COLS = {}
_c = 0
for _nm, _n in _VEC_ORDER:
    VEC_COLS[_nm] = (_c, _n)
    _c += _n
VEC_TOT = _c


def perm_q(x):
    idx = []
    for h in range(16):
        idx += [64 * h + 2 * j for j in range(32)]
    for h in range(16):
        idx += [64 * h + 2 * j + 1 for j in range(32)]
    return np.ascontiguousarray(np.take(x, np.array(idx), axis=-1))


def perm_k(x):
    idx = []
    for h in range(4):
        idx += [64 * h + 2 * j for j in range(32)]
    for h in range(4):
        idx += [64 * h + 2 * j + 1 for j in range(32)]
    return np.ascontiguousarray(np.take(x, np.array(idx), axis=-1))


def build_vec_blob(inp):
    v = np.zeros((128, VEC_TOT), np.float32)

    def put(name, arr):
        c0, n = VEC_COLS[name]
        v[:, c0:c0 + n] = arr

    def pt(x):
        n = x.shape[0] // 128
        return np.ascontiguousarray(x.reshape(n, 128).T)

    put("ones", np.ones((128, 128), np.float32))
    put("eps", np.full((128, 1), 1e-5, np.float32))
    put("kinv1", np.full((128, 1), 1.0 / 1024.0, np.float32))
    put("kh2", np.full((128, 1), 0.5 / 4096.0, np.float32))
    put("b1", pt(np.asarray(inp["L0_ffn_b1"])))
    put("b2", pt(np.asarray(inp["L0_ffn_b2"])))
    put("bq", pt(perm_q(np.asarray(inp["L0_attn_bq"]))))
    put("bk", pt(perm_k(np.asarray(inp["L0_attn_bk"]))))
    put("bo", pt(np.asarray(inp["L0_attn_bo"])))
    put("gam", pt(np.asarray(inp["L1_ffn_ln_gamma"])))
    put("bet", pt(np.asarray(inp["L1_ffn_ln_beta"])))
    put("fc1b", pt(np.asarray(inp["L1_ffn_fc1_bias"])))
    put("fc2b", pt(np.asarray(inp["L1_ffn_fc2_bias"])))
    cw = np.asarray(inp["L1_attn_conv_w"]).reshape(16, 128, 4)
    put("cw", np.ascontiguousarray(cw.transpose(1, 0, 2).reshape(128, 64)))
    put("convb", pt(np.asarray(inp["L1_attn_conv_b"])))
    put("dtb", pt(np.asarray(inp["L1_attn_dt_proj_b"])))
    put("dd", pt(np.asarray(inp["L1_attn_D"])))
    put("mg", pt(np.asarray(inp["L1_attn_ln_g"])))
    put("mb", pt(np.asarray(inp["L1_attn_ln_b"])))
    al = np.asarray(inp["L1_attn_A_log"]).reshape(16, 128, 16)
    put("alog", np.ascontiguousarray(al.transpose(1, 0, 2).reshape(128, 256)))
    hs = np.zeros((128, 4), np.float32)
    for p in range(128):
        hs[p, p // 32] = 1.0
    put("hsel", hs)
    i4 = np.zeros((128, 4), np.float32)
    i4[0:4, 0:4] = np.eye(4)
    put("id4", i4)
    return v


class Kern:
    def __init__(self, stages):
        self.stages = stages
        nc = bass.Bass("TRN2", target_bir_lowering=False)
        self.nc = nc
        self.b = B(nc)
        self.xin = nc.dram_tensor("xin", [D, T], F32, kind="ExternalInput").ap()
        self.xout = nc.dram_tensor("xout", [D, T], F32, kind="ExternalOutput").ap()
        self.xmid = nc.dram_tensor("xmid", [D, T], F32, kind="Internal").ap()
        self.xnspill = nc.dram_tensor("xnspill", [D, T], BF16, kind="Internal").ap()
        self.xnmsp = nc.dram_tensor("xnmsp", [D, T], BF16, kind="Internal").ap()
        self.vec_h = nc.dram_tensor("vec", [128, VEC_TOT], F32, kind="ExternalInput").ap()
        self.dbg = nc.dram_tensor("dbg", [128, 32 * TC], F32, kind="ExternalOutput").ap()
        self.dbg_used = False
        w = {}

        def dram(name, shape, dt):
            w[name] = nc.dram_tensor(name, shape, dt, kind="ExternalInput").ap()
        if any(s.startswith("ffn0") for s in stages):
            dram("w1T", [D, 4096], F32)
            dram("w2T", [4096, D], F32)
        if "attn0" in stages:
            dram("wq", [D, D], F32)
            dram("wk", [D, 256], F32)
            dram("wv", [D, 256], F32)
            dram("bv", [1, 256], F32)
            dram("wo", [D, D], F32)
            dram("rotq", [128, 2 * T], F32)
            dram("rotk", [128, 2 * T], F32)
            dram("cmask", [128, 4 * TC], BF16)
            dram("identb", [128, 128], BF16)
            dram("onesbf", [1, 128], BF16)
        if "mamba1" in stages:
            dram("ipw", [D, 4096], F32)
            dram("xpw", [2048, 96], F32)
            dram("dpw", [64, 2048], F32)
            dram("opw", [2048, D], F32)
            dram("rs", [1, 1], F32)
            self.ipwc = nc.dram_tensor("ipwc", [D, 4096], BF16, kind="Internal").ap()
            self.opwc = nc.dram_tensor("opwc", [2048, D], BF16, kind="Internal").ap()
        if any(s.startswith("ffn1") for s in stages):
            dram("q1", [D, 4096], I32)
            dram("q2", [4096, D], I32)
            dram("sc1", [8, 4096], F32)
            dram("sc2", [32, D], F32)
            dram("rowsel8", [8, 8 * 128], F32)
        if any(s.startswith("ffn1") for s in stages):
            dram("rowsel32", [32, 32 * 128], F32)
        self.w = w

        b = self.b
        self.vec, self._vg = b.sbuf("vecsb", [128, VEC_TOT], F32)
        self.s_ld = b.sem("s_ld")
        self.s_gld = b.sem("s_gld")
        self.s_st = b.sem("s_st")
        self.s_t = b.sem("s_t")
        self.s_v = b.sem("s_v")
        self.s_s = b.sem("s_s")
        self.s_x = b.sem("s_x")
        self.sx2 = [b.sem("sx2a"), b.sem("sx2b")]
        self.sw2 = [b.sem("sw2a"), b.sem("sw2b")]
        self.sq3 = [b.sem(f"sq3{i}") for i in range(3)]
        self.s_p = b.sem("s_p")
        self.s_sp1 = b.sem("s_sp1")
        self.s_sp2 = b.sem("s_sp2")

    def vsl(self, name, c0=0, n=None):
        base, tot = VEC_COLS[name]
        if n is None:
            n = tot
        return self.vec[:, base + c0: base + c0 + n]

    def emit(self):
        b = self.b
        b.op("sync", lambda e: e.dma_start(out=self.vec[:], in_=self.vec_h),
             inc=self.s_ld, k=16)
        b.wait("sync", self.s_ld.at())
        b.barrier()
        cur_in = self.xin
        for i, st in enumerate(self.stages):
            last = (i == len(self.stages) - 1)
            cur_out = self.xout if last else self.xmid
            if st in ("ffn0a", "ffn0b"):
                self.emit_ffn0(cur_in, cur_out)
            elif st in ("ffn1a", "ffn1b"):
                self.emit_ffn1(cur_in, cur_out)
            elif st == "attn0":
                self.emit_attn(cur_in, cur_out)
            elif st == "mamba1":
                self.emit_mamba(cur_in, cur_out)
            else:
                raise ValueError(st)
            b.barrier()
            cur_in = cur_out
        b.wait("sync", self.s_st.at())
        b.finalize()

    def emit_ln(self, xsrc, xn_sb, affine=None, spill=None, affine_spill=None):
        b = self.b
        xt, g1 = b.sbuf("lnx", [128, 2 * T], F32)
        sq, g2 = b.sbuf("lnsq", [128, 2 * T], F32)
        rows, g3 = b.sbuf("lnrows", [1, 8 * T], F32)
        bc, g4 = b.sbuf("lnbc", [128, 3 * T], F32)
        t1s, g5 = b.sbuf("lnt1", [128, T], F32)
        xm, g6 = b.sbuf("lnxm", [128, T], BF16)
        ones1 = self.vsl("ones", 0, 1)
        onesr = self.vec[0:1, 0:128]
        eps_r = self.vec[0:1, VEC_COLS["eps"][0]:VEC_COLS["eps"][0] + 1]

        def row(r):
            return rows[0:1, r * T:(r + 1) * T]

        def rowc(r, cs):
            return rows[0:1, r * T:(r + 1) * T][:, cs]

        ps_sx, gpx = b.psum("lnpsx", [1, T])
        ps_sq, gpq = b.psum("lnpsq", [1, T])
        sq_done = [0] * NJ
        mm_done = [0] * NJ
        xld = [0] * NJ
        for j in range(NJ):
            wl = []
            if j >= 2:
                wl = [(self.s_s, sq_done[j - 2]), (self.s_t, mm_done[j - 2])]
            b.op("sync", lambda e, j=j: e.dma_start(
                out=xt[:, (j % 2) * T:(j % 2 + 1) * T],
                in_=xsrc[j * 128:(j + 1) * 128, :]), waits=wl,
                inc=self.sx2[j % 2], k=16)
            xld[j] = self.sx2[j % 2].n
            wl = [self.sx2[j % 2].at()]
            if j >= 2:
                wl.append((self.s_t, mm_done[j - 2]))
            b.op("scalar", lambda e, j=j: e.activation(
                out=sq[:, (j % 2) * T:(j % 2 + 1) * T],
                in_=xt[:, (j % 2) * T:(j % 2 + 1) * T], func=AF.Square),
                waits=wl, inc=self.s_s)
            sq_done[j] = self.s_s.n
            for c in range(NTC):
                cs = slice(c * TC, (c + 1) * TC)
                b.op("tensor", lambda e, j=j, cs=cs: e.matmul(
                    ps_sx[0:1, cs], ones1,
                    xt[:, (j % 2) * T:(j % 2 + 1) * T][:, cs],
                    start=(j == 0), stop=(j == NJ - 1)),
                    waits=[(self.sx2[j % 2], xld[j])] if c == 0 else [])
                b.op("tensor", lambda e, j=j, cs=cs: e.matmul(
                    ps_sq[0:1, cs], ones1,
                    sq[:, (j % 2) * T:(j % 2 + 1) * T][:, cs],
                    start=(j == 0), stop=(j == NJ - 1)),
                    waits=[(self.s_s, sq_done[j])] if c == 0 else [],
                    inc=self.s_t if c == NTC - 1 else None)
            mm_done[j] = self.s_t.n
        stats_done = self.s_t.n

        b.op("vector", lambda e: e.tensor_scalar(
            out=row(0), in0=ps_sx[0:1, :], scalar1=1.0 / D, scalar2=None,
            op0=OP.mult), waits=[(self.s_t, stats_done)])
        b.op("vector", lambda e: e.tensor_scalar(
            out=row(1), in0=ps_sq[0:1, :], scalar1=1.0 / D, scalar2=None,
            op0=OP.mult))
        b.op("vector", lambda e: e.tensor_tensor(
            out=row(2), in0=row(0), in1=row(0), op=OP.mult))
        b.op("vector", lambda e: e.tensor_tensor(
            out=row(1), in0=row(1), in1=row(2), op=OP.subtract),
            inc=self.s_v)
        b.op("scalar", lambda e: e.activation(
            out=row(2), in_=row(1), func=AF.Sqrt, bias=eps_r),
            waits=[self.s_v.at()], inc=self.s_s)
        b.op("vector", lambda e: e.reciprocal(out=row(3), in_=row(2)),
             waits=[self.s_s.at()], inc=self.s_v)
        if affine is not None:
            b.op("vector", lambda e: e.tensor_tensor(
                out=row(2), in0=row(1), in1=row(3), op=OP.mult))
            b.op("vector", lambda e: e.tensor_tensor(
                out=row(2), in0=row(2), in1=row(3), op=OP.mult),
                inc=self.s_v)
            b.op("scalar", lambda e: e.activation(
                out=row(5), in_=row(2), func=AF.Sqrt, bias=eps_r),
                waits=[self.s_v.at()], inc=self.s_s)
            b.op("vector", lambda e: e.reciprocal(out=row(6), in_=row(5)),
                 waits=[self.s_s.at()])
            b.op("vector", lambda e: e.tensor_tensor(
                out=row(4), in0=row(3), in1=row(6), op=OP.mult),
                inc=self.s_v)
        rows_done = self.s_v.n
        b.free(gpq, gpx)

        psbc, gpb = b.psum("lnpsb", [128, TC])
        nb = 3 if affine is not None else 2
        ev_done = 0
        for i in range(nb):
            ridx = (0, 3, 4)[i]
            for c in range(NTC):
                cs = slice(c * TC, (c + 1) * TC)
                b.op("tensor", lambda e, ridx=ridx, cs=cs: e.matmul(
                    psbc[:, :], onesr, rowc(ridx, cs), start=True, stop=True),
                    waits=[(self.s_v, max(rows_done, ev_done))], inc=self.s_t)
                b.op("vector", lambda e, i=i, cs=cs: e.tensor_copy(
                    out=bc[:, i * T:(i + 1) * T][:, cs], in_=psbc[:, :]),
                    waits=[self.s_t.at()], inc=self.s_v)
                ev_done = self.s_v.n
        bcast_done = self.s_v.n
        b.free(gpb)

        gcol, bcol = (None, None)
        if affine is not None:
            gcol, bcol = affine
        v_used = [0] * NJ
        spill_done = [0] * NJ
        aff_done = [0] * NJ
        for j in range(NJ):
            wl = [(self.s_v, bcast_done)]
            if j >= 2:
                wl.append((self.s_v, v_used[j - 2]))
            b.op("sync", lambda e, j=j: e.dma_start(
                out=xt[:, (j % 2) * T:(j % 2 + 1) * T],
                in_=xsrc[j * 128:(j + 1) * 128, :]), waits=wl,
                inc=self.sx2[j % 2], k=16)
            b.op("vector", lambda e, j=j: e.tensor_tensor(
                out=t1s[:, :], in0=xt[:, (j % 2) * T:(j % 2 + 1) * T],
                in1=bc[:, 0:T], op=OP.subtract), waits=[self.sx2[j % 2].at()])
            if affine is None:
                b.op("vector", lambda e, j=j: e.tensor_tensor(
                    out=xn_sb[:, j * T:(j + 1) * T], in0=t1s[:, :],
                    in1=bc[:, T:2 * T], op=OP.mult), inc=self.s_v)
            else:
                wl = [(self.s_sp1, spill_done[j - 1])] if j >= 1 else []
                b.op("vector", lambda e, j=j: e.tensor_tensor(
                    out=xm[:, :], in0=t1s[:, :], in1=bc[:, T:2 * T], op=OP.mult),
                    waits=wl, inc=self.s_v)
                sp_mark = self.s_v.n
                b.op("sync", lambda e, j=j: e.dma_start(
                    out=spill[j * 128:(j + 1) * 128, :], in_=xm[:, :]),
                    waits=[(self.s_v, sp_mark)], inc=self.s_sp1, k=16)
                spill_done[j] = self.s_sp1.n
                wl = [(self.s_s, aff_done[j - 1])] if j >= 1 else []
                b.op("vector", lambda e, j=j: e.tensor_tensor(
                    out=sq[:, 0:T], in0=t1s[:, :], in1=bc[:, 2 * T:3 * T],
                    op=OP.mult), waits=wl, inc=self.s_v)
                sc_mark = self.s_v.n
                if affine_spill is None:
                    b.op("scalar", lambda e, j=j: e.activation(
                        out=xn_sb[:, j * T:(j + 1) * T], in_=sq[:, 0:T],
                        func=AF.Identity,
                        scale=self.vsl(gcol, j, 1), bias=self.vsl(bcol, j, 1)),
                        waits=[(self.s_v, sc_mark)], inc=self.s_s)
                else:
                    wl2 = [(self.s_v, sc_mark)]
                    if j >= 2:
                        wl2.append((self.s_sp2, self._affsp[j - 2]))
                    b.op("scalar", lambda e, j=j: e.activation(
                        out=xn_sb[:, (j % 2) * T:(j % 2 + 1) * T], in_=sq[:, 0:T],
                        func=AF.Identity,
                        scale=self.vsl(gcol, j, 1), bias=self.vsl(bcol, j, 1)),
                        waits=wl2, inc=self.s_s)
                    mk2 = self.s_s.n
                    b.op("sync", lambda e, j=j: e.dma_start(
                        out=affine_spill[j * 128:(j + 1) * 128, :],
                        in_=xn_sb[:, (j % 2) * T:(j % 2 + 1) * T]),
                        waits=[(self.s_s, mk2)], inc=self.s_sp2, k=16)
                    if not hasattr(self, "_affsp"):
                        self._affsp = {}
                    self._affsp[j] = self.s_sp2.n
                aff_done[j] = self.s_s.n
            v_used[j] = self.s_v.n
        self.ln_xn_done_v = self.s_v.n
        self.ln_xn_done_s = self.s_s.n
        b.free(g6, g5, g4, g3, g2, g1)

    def emit_ffn0(self, xsrc, xdst):
        b = self.b
        xn, gxn = b.sbuf("f0xn", [128, NJ * T], BF16)
        self.emit_ln(xsrc, xn)
        b.barrier()
        ws1, gws1 = b.sbuf("f0ws1", [128, NJ * 4096], BF16)
        ws2, gws2 = b.sbuf("f0ws2", [128, 32 * 1024], BF16)
        al_sb, gal = b.sbuf("f0al", [128, 40], F32)
        hb2, ghb2 = b.sbuf("f0hb2", [128, 8], F32)
        psal, gpal = b.psum("f0pal", [128, 40])
        kinv1 = self.vsl("kinv1")
        kh2 = self.vsl("kh2")
        ones1 = self.vsl("ones", 0, 1)
        w1T, w2T = self.w["w1T"], self.w["w2T"]

        wtmp, gwt = b.sbuf("f0wt", [128, 2 * 4096], F32)
        alm = [0] * 32
        for j in range(NJ):
            wl = [] if j < 2 else [(self.s_t, alm[j - 2])]
            b.op("sync", lambda e, j=j: e.dma_start(
                out=wtmp[:, (j % 2) * 4096:(j % 2 + 1) * 4096],
                in_=w1T[j * 128:(j + 1) * 128, :]), waits=wl,
                inc=self.sw2[j % 2], k=16)
            ld = self.sw2[j % 2].n
            b.op("scalar", lambda e, j=j: e.activation(
                out=ws1[:, j * 4096:(j + 1) * 4096],
                in_=wtmp[:, (j % 2) * 4096:(j % 2 + 1) * 4096], func=AF.Sign),
                waits=[(self.sw2[j % 2], ld)], inc=self.s_s)
            b.op("scalar", lambda e, j=j: e.activation(
                out=wtmp[:, (j % 2) * 4096:(j % 2 + 1) * 4096],
                in_=wtmp[:, (j % 2) * 4096:(j % 2 + 1) * 4096], func=AF.Abs),
                inc=self.s_s)
            ab = self.s_s.n
            for fc in range(32):
                b.op("tensor", lambda e, j=j, fc=fc: e.matmul(
                    psal[:, fc:fc + 1],
                    wtmp[:, (j % 2) * 4096 + fc * 128:(j % 2) * 4096 + (fc + 1) * 128],
                    self._ones_col(),
                    start=(j == 0 and fc == 0), stop=False),
                    waits=[(self.s_s, ab)] if fc == 0 else [],
                    inc=self.s_t if fc == 31 else None)
            alm[j] = self.s_t.n
        sgn1_done = self.s_s.n
        w1_al_done = self.s_t.n
        b.free(gwt)

        wt2, gwt2 = b.sbuf("f0wt2", [128, 2 * 1024], F32)
        alm2 = [0] * 32
        for j2 in range(32):
            wl = [(self.s_t, alm[NJ - 1]), (self.s_s, sgn1_done)] if j2 < 2 \
                else [(self.s_t, alm2[j2 - 2])]
            b.op("sync", lambda e, j2=j2: e.dma_start(
                out=wt2[:, (j2 % 2) * 1024:(j2 % 2 + 1) * 1024],
                in_=w2T[j2 * 128:(j2 + 1) * 128, :]), waits=wl,
                inc=self.sw2[j2 % 2], k=16)
            ld = self.sw2[j2 % 2].n
            b.op("scalar", lambda e, j2=j2: e.activation(
                out=ws2[:, j2 * 1024:(j2 + 1) * 1024],
                in_=wt2[:, (j2 % 2) * 1024:(j2 % 2 + 1) * 1024], func=AF.Sign),
                waits=[(self.sw2[j2 % 2], ld)], inc=self.s_s)
            b.op("scalar", lambda e, j2=j2: e.activation(
                out=wt2[:, (j2 % 2) * 1024:(j2 % 2 + 1) * 1024],
                in_=wt2[:, (j2 % 2) * 1024:(j2 % 2 + 1) * 1024], func=AF.Abs),
                inc=self.s_s)
            ab = self.s_s.n
            for oc in range(8):
                b.op("tensor", lambda e, j2=j2, oc=oc: e.matmul(
                    psal[:, 32 + oc:33 + oc],
                    wt2[:, (j2 % 2) * 1024 + oc * 128:(j2 % 2) * 1024 + (oc + 1) * 128],
                    self._ones_col(),
                    start=False, stop=False),
                    waits=[(self.s_s, ab)] if oc == 0 else [],
                    inc=self.s_t if oc == 7 else None)
            alm2[j2] = self.s_t.n
        sgn2_done = self.s_s.n
        al_mm_done = self.s_t.n
        b.free(gwt2)
        b.op("vector", lambda e: e.tensor_scalar(
            out=al_sb[:, 0:32], in0=psal[:, 0:32], scalar1=1.0 / 1024.0,
            scalar2=None, op0=OP.mult), waits=[(self.s_t, al_mm_done)])
        b.op("vector", lambda e: e.tensor_scalar(
            out=al_sb[:, 32:40], in0=psal[:, 32:40], scalar1=0.5 / 4096.0,
            scalar2=None, op0=OP.mult))
        b.op("vector", lambda e: e.tensor_scalar(
            out=hb2[:, :], in0=self.vsl("b2"), scalar1=0.5, scalar2=None,
            op0=OP.mult), inc=self.s_v)
        al_done = self.s_v.n

        hbuf, ghb = b.sbuf("f0h", [128, 32 * TC], BF16)
        xre, gxr = b.sbuf("f0xr", [128, 2 * TC], F32)
        zt, gzt = b.sbuf("f0z", [128, 2 * TC], F32)
        psh = [b.psum(f"f0ph{i}", [128, TC]) for i in range(4)]
        psz = [b.psum(f"f0pz{i}", [128, TC]) for i in range(2)]
        hev = {}
        mm2m = {}
        zev = {}
        xadd = {}
        xldm = {}
        first = True
        for c in range(NTC):
            for fc in range(32):
                bank = fc % 4
                wl = []
                if first:
                    wl = [(self.s_s, sgn2_done), (self.s_v, self.ln_xn_done_v),
                          (self.s_s, self.ln_xn_done_s)]
                    first = False
                if c > 0 and fc < 4:
                    wl.append((self.s_s, hev[(c - 1, fc + 28)]))
                if fc >= 4:
                    wl.append((self.s_s, hev[(c, fc - 4)]))
                for j in range(NJ):
                    b.op("tensor", lambda e, j=j, fc=fc, c=c, bank=bank: e.matmul(
                        psh[bank][0][:, :],
                        ws1[:, j * 4096 + fc * 128:j * 4096 + (fc + 1) * 128],
                        xn[:, j * T + c * TC:j * T + (c + 1) * TC],
                        start=(j == 0), stop=(j == NJ - 1)),
                        waits=wl if j == 0 else [],
                        inc=self.s_t if j == NJ - 1 else None)
                mmark = self.s_t.n
                wl2 = [(self.s_t, mmark)]
                if c == 0 and fc == 0:
                    wl2.append((self.s_v, al_done))
                if c > 0:
                    wl2.append((self.s_t, mm2m[(c - 1, 7)]))
                b.op("scalar", lambda e, fc=fc, c=c, bank=bank: e.activation(
                    out=hbuf[:, fc * TC:(fc + 1) * TC],
                    in_=psh[bank][0][:, :], func=AF.Gelu,
                    scale=al_sb[:, fc:fc + 1], bias=self.vsl("b1", fc, 1)),
                    waits=wl2, inc=self.s_s)
                hev[(c, fc)] = self.s_s.n
            for oc in range(NJ):
                it = c * NJ + oc
                bank = oc % 2
                wl = []
                if oc == 0:
                    wl.append((self.s_s, hev[(c, 31)]))
                if it >= 2:
                    wl.append((self.s_s, zev[divmod(it - 2, NJ)]))
                wlx = []
                if it >= 2:
                    wlx.append((self.s_v, xadd[divmod(it - 2, NJ)]))
                else:
                    wlx.append((self.s_t, al_mm_done))
                b.op("sync", lambda e, c=c, oc=oc, it=it: e.dma_start(
                    out=xre[:, (it % 2) * TC:(it % 2 + 1) * TC],
                    in_=xsrc[oc * 128:(oc + 1) * 128, c * TC:(c + 1) * TC]),
                    waits=wlx, inc=self.sx2[it % 2], k=16)
                xldm[(c, oc)] = self.sx2[it % 2].n
                for j2 in range(32):
                    b.op("tensor", lambda e, j2=j2, oc=oc, c=c, bank=bank: e.matmul(
                        psz[bank][0][:, :],
                        ws2[:, j2 * 1024 + oc * 128:j2 * 1024 + (oc + 1) * 128],
                        hbuf[:, j2 * TC:(j2 + 1) * TC],
                        start=(j2 == 0), stop=(j2 == 31)),
                        waits=wl if j2 == 0 else [],
                        inc=self.s_t if j2 == 31 else None)
                mm2m[(c, oc)] = self.s_t.n
                b.op("scalar", lambda e, c=c, oc=oc, it=it, bank=bank: e.activation(
                    out=zt[:, (it % 2) * TC:(it % 2 + 1) * TC],
                    in_=psz[bank][0][:, :], func=AF.Identity,
                    scale=al_sb[:, 32 + oc:33 + oc], bias=hb2[:, oc:oc + 1]),
                    waits=[(self.s_t, mm2m[(c, oc)])] +
                          ([(self.s_v, xadd[divmod(it - 2, NJ)])] if it >= 2 else []),
                    inc=self.s_s)
                zev[(c, oc)] = self.s_s.n
                b.op("vector", lambda e, c=c, oc=oc, it=it: e.tensor_tensor(
                    out=xre[:, (it % 2) * TC:(it % 2 + 1) * TC],
                    in0=xre[:, (it % 2) * TC:(it % 2 + 1) * TC],
                    in1=zt[:, (it % 2) * TC:(it % 2 + 1) * TC], op=OP.add),
                    waits=[(self.s_s, zev[(c, oc)]),
                           (self.sx2[it % 2], xldm[(c, oc)])],
                    inc=self.s_v)
                xadd[(c, oc)] = self.s_v.n
                b.op("sync", lambda e, c=c, oc=oc, it=it: e.dma_start(
                    out=xdst[oc * 128:(oc + 1) * 128, c * TC:(c + 1) * TC],
                    in_=xre[:, (it % 2) * TC:(it % 2 + 1) * TC]),
                    waits=[(self.s_v, xadd[(c, oc)])], inc=self.s_st, k=16)
        for h, g in reversed(psh + psz):
            b.free(g)
        b.free(gpal)
        b.free(gzt, gxr, ghb)
        b.free(ghb2, gal, gws2, gws1, gxn)

    def _ones_col(self):
        return self.vsl("ones", 0, 1)

    def emit_ffn1(self, xsrc, xdst):
        b = self.b
        xn, gxn = b.sbuf("f1xn", [128, NJ * T], BF16)
        self.emit_ln(xsrc, xn, affine=("gam", "bet"), spill=self.xnspill)
        b.barrier()
        ws1, gws1 = b.sbuf("f1w1", [128, NJ * 4096], BF16)
        ws2, gws2 = b.sbuf("f1w2", [128, 32 * 1024], BF16)
        qb2, gqb2 = b.sbuf("f1qb2", [128, 8], F32)
        q1, q2 = self.w["q1"], self.w["q2"]
        sel8, sel32 = self.w["rowsel8"], self.w["rowsel32"]

        sc1sb, gs1 = b.sbuf("f1s1", [8, 4096], F32)
        s8sb, gs8 = b.sbuf("f1s8", [8, 8 * 128], F32)
        b.op("sync", lambda e: e.dma_start(out=sc1sb[:], in_=self.w["sc1"]),
             inc=self.s_ld, k=16)
        b.op("sync", lambda e: e.dma_start(out=s8sb[:], in_=sel8),
             inc=self.s_ld, k=16)
        sc_ld = self.s_ld.n
        psd = [b.psum(f"f1pd{i}", [128, TC]) for i in range(2)]

        qtmp, gqt = b.sbuf("f1qt", [128, 4096], F32)
        dq = {}
        for g_ in range(8):
            wl = [(self.s_ld, sc_ld)] if g_ == 0 else \
                 [(self.s_v, dq[(g_ - 1, 7)])]
            b.op("gpsimd", lambda e, g_=g_: e.dma_start(
                out=qtmp[:, :], in_=q1[g_ * 128:(g_ + 1) * 128, :]),
                waits=wl, inc=self.s_gld, k=16)
            ld = self.s_gld.n
            for ch in range(8):
                bank = ch % 2
                wl = [(self.s_v, dq[(g_, ch - 2)])] if ch >= 2 else (
                    [(self.s_v, dq[(g_ - 1, ch + 6)])] if g_ > 0 else
                    [(self.s_ld, sc_ld)])
                b.op("tensor", lambda e, g_=g_, ch=ch, bank=bank: e.matmul(
                    psd[bank][0][:, :], s8sb[:, g_ * 128:(g_ + 1) * 128],
                    sc1sb[:, ch * TC:(ch + 1) * TC], start=True, stop=True),
                    waits=wl + ([(self.s_gld, ld)] if ch == 0 else []),
                    inc=self.s_t)
                mk = self.s_t.n
                b.op("vector", lambda e, g_=g_, ch=ch, bank=bank: e.scalar_tensor_tensor(
                    out=ws1[:, g_ * 4096 + ch * TC:g_ * 4096 + (ch + 1) * TC],
                    in0=qtmp[:, ch * TC:(ch + 1) * TC], scalar=-128.0,
                    in1=psd[bank][0][:, :], op0=OP.add, op1=OP.mult),
                    waits=[(self.s_t, mk)], inc=self.s_v)
                dq[(g_, ch)] = self.s_v.n
        b.free(gqt, gs8, gs1)
        dq1_done = self.s_v.n

        sc2sb, gs2 = b.sbuf("f1s2", [32, 1024], F32)
        s32sb, gs32 = b.sbuf("f1s32", [32, 32 * 128], F32)
        b.op("sync", lambda e: e.dma_start(out=sc2sb[:], in_=self.w["sc2"]),
             waits=[(self.s_v, dq1_done), (self.s_t, self.s_t.n)],
             inc=self.s_ld, k=16)
        b.op("sync", lambda e: e.dma_start(out=s32sb[:], in_=sel32),
             inc=self.s_ld, k=16)
        sc_ld2 = self.s_ld.n
        qtmp2, gqt2 = b.sbuf("f1qt2", [128, 2 * 1024], F32)
        dq2 = {}
        for g_ in range(32):
            wl = [(self.s_ld, sc_ld2)] if g_ == 0 else []
            if g_ >= 2:
                wl.append((self.s_v, dq2[(g_ - 2, 1)]))
            b.op("gpsimd", lambda e, g_=g_: e.dma_start(
                out=qtmp2[:, (g_ % 2) * 1024:(g_ % 2 + 1) * 1024],
                in_=q2[g_ * 128:(g_ + 1) * 128, :]),
                waits=wl, inc=self.sw2[g_ % 2], k=16)
            ld = self.sw2[g_ % 2].n
            for ch in range(2):
                bank = ch % 2
                wl = []
                if g_ > 0:
                    wl.append((self.s_v, dq2[(g_ - 1, ch)]))
                else:
                    wl.append((self.s_ld, sc_ld2))
                b.op("tensor", lambda e, g_=g_, ch=ch, bank=bank: e.matmul(
                    psd[bank][0][:, :], s32sb[:, g_ * 128:(g_ + 1) * 128],
                    sc2sb[:, ch * TC:(ch + 1) * TC], start=True, stop=True),
                    waits=wl + ([(self.sw2[g_ % 2], ld)] if ch == 0 else []),
                    inc=self.s_t)
                mk = self.s_t.n
                b.op("vector", lambda e, g_=g_, ch=ch, bank=bank: e.scalar_tensor_tensor(
                    out=ws2[:, g_ * 1024 + ch * TC:g_ * 1024 + (ch + 1) * TC],
                    in0=qtmp2[:, (g_ % 2) * 1024 + ch * TC:(g_ % 2) * 1024 + (ch + 1) * TC],
                    scalar=-128.0, in1=psd[bank][0][:, :], op0=OP.add, op1=OP.mult),
                    waits=[(self.s_t, mk)], inc=self.s_v)
                dq2[(g_, ch)] = self.s_v.n
        b.free(gqt2, gs32, gs2)
        dq2_done = self.s_v.n
        b.op("vector", lambda e: e.tensor_scalar(
            out=qb2[:, :], in0=self.vsl("fc2b"), scalar1=0.25, scalar2=None,
            op0=OP.mult), inc=self.s_v)
        qb2_done = self.s_v.n

        hbuf, ghb = b.sbuf("f1h", [128, 32 * TC], BF16)
        xre, gxr = b.sbuf("f1xr", [128, 2 * TC], F32)
        zt, gzt = b.sbuf("f1z", [128, 2 * TC], F32)
        xnm, gxm = b.sbuf("f1xm", [128, 2 * TC], BF16)
        psh = [b.psum(f"f1ph{i}", [128, TC]) for i in range(4)]
        psz = [b.psum(f"f1pz{i}", [128, TC]) for i in range(2)]
        hev = {}
        mm2m = {}
        zev = {}
        xadd = {}
        xldm = {}
        first = True
        for c in range(NTC):
            for fc in range(32):
                bank = fc % 4
                wl = []
                if first:
                    wl = [(self.s_v, dq2_done), (self.s_v, self.ln_xn_done_v),
                          (self.s_s, self.ln_xn_done_s)]
                    first = False
                if c > 0 and fc < 4:
                    wl.append((self.s_s, hev[(c - 1, fc + 28)]))
                if fc >= 4:
                    wl.append((self.s_s, hev[(c, fc - 4)]))
                for j in range(NJ):
                    b.op("tensor", lambda e, j=j, fc=fc, c=c, bank=bank: e.matmul(
                        psh[bank][0][:, :],
                        ws1[:, j * 4096 + fc * 128:j * 4096 + (fc + 1) * 128],
                        xn[:, j * T + c * TC:j * T + (c + 1) * TC],
                        start=(j == 0), stop=(j == NJ - 1)),
                        waits=wl if j == 0 else [],
                        inc=self.s_t if j == NJ - 1 else None)
                mmark = self.s_t.n
                wl2 = [(self.s_t, mmark)]
                if c > 0:
                    wl2.append((self.s_t, mm2m[(c - 1, 7)]))
                b.op("scalar", lambda e, fc=fc, c=c, bank=bank: e.activation(
                    out=hbuf[:, fc * TC:(fc + 1) * TC],
                    in_=psh[bank][0][:, :], func=AF.Silu,
                    bias=self.vsl("fc1b", fc, 1)),
                    waits=wl2, inc=self.s_s)
                hev[(c, fc)] = self.s_s.n
            for oc in range(NJ):
                it = c * NJ + oc
                bank = oc % 2
                wl = []
                if oc == 0:
                    wl.append((self.s_s, hev[(c, 31)]))
                if it >= 2:
                    wl.append((self.s_s, zev[divmod(it - 2, NJ)]))
                wlx = []
                if it >= 2:
                    wlx.append((self.s_v, xadd[divmod(it - 2, NJ)]))
                else:
                    wlx.append((self.s_v, dq2_done))
                b.op("sync", lambda e, c=c, oc=oc, it=it: e.dma_start(
                    out=xre[:, (it % 2) * TC:(it % 2 + 1) * TC],
                    in_=xsrc[oc * 128:(oc + 1) * 128, c * TC:(c + 1) * TC]),
                    waits=wlx, inc=self.sx2[it % 2], k=16)
                b.op("sync", lambda e, c=c, oc=oc, it=it: e.dma_start(
                    out=xnm[:, (it % 2) * TC:(it % 2 + 1) * TC],
                    in_=self.xnspill[oc * 128:(oc + 1) * 128, c * TC:(c + 1) * TC]),
                    inc=self.sx2[it % 2], k=16)
                xldm[(c, oc)] = self.sx2[it % 2].n
                for j2 in range(32):
                    b.op("tensor", lambda e, j2=j2, oc=oc, c=c, bank=bank: e.matmul(
                        psz[bank][0][:, :],
                        ws2[:, j2 * 1024 + oc * 128:j2 * 1024 + (oc + 1) * 128],
                        hbuf[:, j2 * TC:(j2 + 1) * TC],
                        start=(j2 == 0), stop=(j2 == 31)),
                        waits=wl if j2 == 0 else [],
                        inc=self.s_t if j2 == 31 else None)
                mm2m[(c, oc)] = self.s_t.n
                b.op("scalar", lambda e, c=c, oc=oc, it=it, bank=bank: e.activation(
                    out=zt[:, (it % 2) * TC:(it % 2 + 1) * TC],
                    in_=psz[bank][0][:, :], func=AF.Identity,
                    scale=0.25, bias=qb2[:, oc:oc + 1]),
                    waits=[(self.s_t, mm2m[(c, oc)]), (self.s_v, qb2_done)] +
                          ([(self.s_v, xadd[divmod(it - 2, NJ)])] if it >= 2 else []),
                    inc=self.s_s)
                zev[(c, oc)] = self.s_s.n
                b.op("vector", lambda e, it=it: e.scalar_tensor_tensor(
                    out=xre[:, (it % 2) * TC:(it % 2 + 1) * TC],
                    in0=xnm[:, (it % 2) * TC:(it % 2 + 1) * TC], scalar=0.5,
                    in1=xre[:, (it % 2) * TC:(it % 2 + 1) * TC],
                    op0=OP.mult, op1=OP.add),
                    waits=[(self.s_s, zev[(c, oc)]),
                           (self.sx2[it % 2], xldm[(c, oc)])])
                b.op("vector", lambda e, it=it: e.tensor_tensor(
                    out=xre[:, (it % 2) * TC:(it % 2 + 1) * TC],
                    in0=xre[:, (it % 2) * TC:(it % 2 + 1) * TC],
                    in1=zt[:, (it % 2) * TC:(it % 2 + 1) * TC], op=OP.add),
                    inc=self.s_v)
                xadd[(c, oc)] = self.s_v.n
                b.op("sync", lambda e, c=c, oc=oc, it=it: e.dma_start(
                    out=xdst[oc * 128:(oc + 1) * 128, c * TC:(c + 1) * TC],
                    in_=xre[:, (it % 2) * TC:(it % 2 + 1) * TC]),
                    waits=[(self.s_v, xadd[(c, oc)])], inc=self.s_st, k=16)
        for h, g in reversed(psd + psh + psz):
            b.free(g)
        b.free(gxm, gzt, gxr, ghb)
        b.free(gqb2, gws2, gws1, gxn)

    def emit_attn(self, xsrc, xdst):
        b = self.b
        w = self.w
        xn, gxn = b.sbuf("atxn", [128, NJ * T], BF16)
        self.emit_ln(xsrc, xn)
        b.barrier()
        NH = 16
        eps1 = self.vec[0:1, VEC_COLS["eps"][0]:VEC_COLS["eps"][0] + 1][0:1, 0:1]
        hsel = self.vsl("hsel")

        qrot, gqr = b.sbuf("atQ", [128, 8 * T], BF16)
        krot, gkr = b.sbuf("atK", [128, 2 * T], BF16)
        vt, gvt = b.sbuf("atV", [128, 16 * 260], BF16)
        bnd, gbn = b.sbuf("atB", [16, T], BF16)
        ctxF, gcf = b.sbuf("atC", [128, 8 * T], BF16)
        idb, gid = b.sbuf("atI", [128, 128], BF16)
        cmk, gcm = b.sbuf("atM", [128, 4 * TC], BF16)
        obf, gob = b.sbuf("atO1", [1, 128], BF16)
        sml, gsm = b.sbuf("atS", [128, 16], F32)

        rq, grq = b.sbuf("atrq", [128, 2 * T], F32)
        rk, grk = b.sbuf("atrk", [128, 2 * T], F32)
        wqsb, gwq = b.sbuf("atwq", [128, 8 * 1024], BF16)
        wksb, gwk = b.sbuf("atwk", [128, 8 * 256], BF16)
        wvsb, gwv = b.sbuf("atwv", [128, 8 * 256], BF16)
        qraw, gqw = b.sbuf("atqr", [128, 8 * TC], F32)
        sqs, gsq = b.sbuf("atsq", [128, TC], F32)
        sqn, gsqn = b.sbuf("atsn", [128, TC], F32)
        n4, gn4 = b.sbuf("atn4", [4, 2 * TC], F32)
        bnd4, gb4 = b.sbuf("atb4", [4, TC], BF16)
        bvb, gbv = b.sbuf("atbv", [1, 256], BF16)
        psp = [b.psum(f"atpp{i}", [128, TC]) for i in range(2)]
        psn, gpn = b.psum("atpn", [4, TC])

        for nm, dst in [("rotq", rq), ("rotk", rk)]:
            b.op("sync", lambda e, nm=nm, dst=dst: e.dma_start(out=dst[:], in_=w[nm]),
                 inc=self.s_ld, k=16)
        b.op("sync", lambda e: e.dma_start(out=idb[:], in_=w["identb"]), inc=self.s_ld, k=16)
        b.op("sync", lambda e: e.dma_start(out=cmk[:], in_=w["cmask"]), inc=self.s_ld, k=16)
        b.op("sync", lambda e: e.dma_start(out=obf[:], in_=w["onesbf"]), inc=self.s_ld, k=16)
        for nm, dst, ncol in [("wq", wqsb, 1024), ("wk", wksb, 256), ("wv", wvsb, 256)]:
            for j in range(NJ):
                b.op("gpsimd", lambda e, nm=nm, dst=dst, ncol=ncol, j=j: e.dma_start(
                    out=dst[:, j * ncol:(j + 1) * ncol],
                    in_=w[nm][j * 128:(j + 1) * 128, :]), inc=self.s_gld, k=16)
        b.op("gpsimd", lambda e: e.dma_start(out=bvb[:], in_=w["bv"]), inc=self.s_gld, k=16)
        ld_t = self.s_ld.n
        gld_t = self.s_gld.n
        b.op("vector", lambda e: e.memset(vt[:, :], 1.0), inc=self.s_v)
        vt_ones = self.s_v.n

        def proj_block(dst_slot, wsb, wstride, oc, c, bias_name, bias_col, bk, wl0):
            cs = slice(c * TC, (c + 1) * TC)
            for j in range(NJ):
                b.op("tensor", lambda e, j=j: e.matmul(
                    psp[bk][0][:, :],
                    wsb[:, j * wstride + oc * 128:j * wstride + (oc + 1) * 128],
                    xn[:, j * T:(j + 1) * T][:, cs],
                    start=(j == 0), stop=(j == NJ - 1)),
                    waits=wl0 if j == 0 else [],
                    inc=self.s_t if j == NJ - 1 else None)
            mk = self.s_t.n
            b.op("scalar", lambda e: e.activation(
                out=qraw[:, dst_slot * TC:(dst_slot + 1) * TC], in_=psp[bk][0][:, :],
                func=AF.Identity, bias=self.vsl(bias_name, bias_col, 1)),
                waits=[(self.s_t, mk)], inc=self.s_s)
            return self.s_s.n

        def rot_pair(src_e, src_o, dst, tabs, c, wl0):
            cs = slice(c * TC, (c + 1) * TC)
            a_tb, b_tb, table = tabs
            b.op("vector", lambda e: e.tensor_tensor(
                out=sqs[:, :], in0=qraw[:, src_e * TC:(src_e + 1) * TC],
                in1=table[:, a_tb * T:(a_tb + 1) * T][:, cs], op=OP.mult), waits=wl0)
            b.op("vector", lambda e: e.tensor_tensor(
                out=rk[:, 0:TC] if table is rq else qraw[:, 2 * TC:3 * TC],
                in0=qraw[:, src_o * TC:(src_o + 1) * TC],
                in1=table[:, b_tb * T:(b_tb + 1) * T][:, cs], op=OP.mult))
            tmp = rk[:, 0:TC] if table is rq else qraw[:, 2 * TC:3 * TC]
            b.op("vector", lambda e, tmp=tmp: e.tensor_tensor(
                out=dst[:, cs], in0=sqs[:, :], in1=tmp, op=OP.subtract), inc=self.s_v)
            return self.s_v.n

        def norms(tile_e_ap, tile_o_ap, c, wl0):
            for o2, src in enumerate((tile_e_ap, tile_o_ap)):
                cs = slice(c * TC, (c + 1) * TC)
                b.op("scalar", lambda e, src=src, cs=cs: e.activation(
                    out=sqn[:, :], in_=src[:, cs], func=AF.Square),
                    waits=(wl0 if o2 == 0 else []) + [(self.s_t, self.s_t.n)],
                    inc=self.s_s)
                sqm = self.s_s.n
                b.op("tensor", lambda e, o2=o2: e.matmul(
                    psn[:, :], hsel, sqn[:, :], start=(o2 == 0), stop=(o2 == 1)),
                    waits=[(self.s_s, sqm)] +
                          ([(self.s_v, self.s_v.n)] if o2 == 0 else []),
                    inc=self.s_t if o2 == 1 else None)
            mk = self.s_t.n
            b.op("vector", lambda e: e.tensor_copy(out=n4[0:4, 0:TC], in_=psn[:, :]),
                 waits=[(self.s_t, mk)], inc=self.s_v)
            b.op("scalar", lambda e: e.activation(
                out=n4[0:4, TC:2 * TC], in_=n4[0:4, 0:TC], func=AF.Sqrt,
                bias=self.vsl("eps")[0:4, 0:1]),
                waits=[self.s_v.at()], inc=self.s_s)
            return self.s_s.n

        ev = {}
        for c in range(NTC):
            for o2 in range(2):
                idx = c * 2 + o2
                wl = []
                if idx == 0:
                    wl = [(self.s_gld, gld_t), (self.s_v, self.ln_xn_done_v)]
                if idx >= 2:
                    wl.append((self.s_s, ev[idx - 2]))
                ev[idx] = proj_block(o2, wksb, 256, o2, c, "bk", o2, o2, wl)
            wl = [(self.s_s, ev[c * 2 + 1]), (self.s_ld, ld_t)]
            rot_pair(0, 1, krot[:, 0:T], (0, 1, rk), c, wl)
            kro = rot_pair(0, 1, krot[:, T:2 * T], (1, 0, rk), c, [])
            nm = norms(krot[:, 0:T], krot[:, T:2 * T], c, [(self.s_v, kro)])
            b.op("vector", lambda e, c=c: e.reduce_max(
                out=sml[0:4, c:c + 1], in_=n4[0:4, TC:2 * TC], axis=AX.X),
                waits=[(self.s_s, nm)], inc=self.s_v)
        b.op("vector", lambda e: e.reduce_max(out=sml[0:4, 4:5], in_=sml[0:4, 0:4],
                                              axis=AX.X), inc=self.s_v)
        mkv = self.s_v.n
        b.op("tensor", lambda e: e.matmul(psn[0:1, 0:4], sml[0:4, 4:5],
                                          self.vsl("id4")[0:4, :], start=True,
                                          stop=True),
             waits=[(self.s_v, mkv)], inc=self.s_t)
        b.op("vector", lambda e: e.tensor_copy(out=sml[0:1, 8:12], in_=psn[0:1, 0:4]),
             waits=[self.s_t.at()], inc=self.s_v)
        b.op("vector", lambda e: e.reduce_max(out=sml[0:1, 5:6], in_=sml[0:1, 8:12],
                                              axis=AX.X))
        b.op("vector", lambda e: e.tensor_scalar(
            out=sml[0:1, 6:7], in0=sml[0:1, 5:6], scalar1=-1.01, scalar2=None,
            op0=OP.mult), inc=self.s_v)
        nmk = self.s_v.n
        b.op("tensor", lambda e: e.matmul(psn[:, 0:1], self.vec[0:1, 0:4],
                                          sml[0:1, 6:7], start=True, stop=True),
             waits=[(self.s_v, nmk)], inc=self.s_t)
        b.op("vector", lambda e: e.tensor_copy(out=sml[0:4, 7:8], in_=psn[:, 0:1]),
             waits=[self.s_t.at()], inc=self.s_v)
        negmk4 = self.s_v.n

        for c in range(NTC):
            qev = {}
            for oc in range(8):
                idx = c * 8 + oc
                wl = []
                if idx >= 2:
                    pass
                if oc >= 2:
                    wl.append((self.s_s, qev[oc - 2]))
                elif c > 0:
                    wl.append((self.s_s, self._qev_prev[oc]))
                qev[oc] = proj_block(oc, wqsb, 1024, oc, c, "bq", oc, oc % 2, wl)
            self._qev_prev = {0: qev[6], 1: qev[7]}
            for i in range(4):
                wl = [(self.s_s, qev[7]), (self.s_ld, ld_t)]
                rot_pair(i, 4 + i, qrot[:, i * T:(i + 1) * T], (0, 1, rq), c, wl)
                rd = rot_pair(i, 4 + i, qrot[:, (4 + i) * T:(5 + i) * T], (1, 0, rq), c, [])
                nm = norms(qrot[:, i * T:(i + 1) * T], qrot[:, (4 + i) * T:(5 + i) * T],
                           c, [(self.s_v, rd)])
                b.op("vector", lambda e: e.tensor_scalar(
                    out=bnd4[0:4, :], in0=n4[0:4, TC:2 * TC], scalar1=sml[0:4, 7:8],
                    scalar2=None, op0=OP.mult),
                    waits=[(self.s_s, nm), (self.s_v, negmk4),
                           (self.s_gld, self.s_gld.n)], inc=self.s_v)
                bm = self.s_v.n
                b.op("gpsimd", lambda e, i=i, c=c: e.dma_start(
                    out=bnd[4 * i:4 * i + 4, c * TC:(c + 1) * TC], in_=bnd4[0:4, :]),
                    waits=[(self.s_v, bm)], inc=self.s_gld, k=16)

        vev = {}
        for t16 in range(16):
            bk = t16 % 2
            wl = [(self.s_v, vt_ones), (self.s_gld, gld_t)]
            if t16 < 2:
                wl.append((self.s_s, self.s_s.n))
            if t16 >= 2:
                wl.append((self.s_v, vev[t16 - 2]))
            for j in range(NJ):
                b.op("tensor", lambda e, j=j, t16=t16, bk=bk: e.matmul(
                    psp[bk][0][:, 0:256],
                    xn[:, j * T + t16 * 128:j * T + (t16 + 1) * 128],
                    wvsb[:, j * 256:(j + 1) * 256], start=(j == 0), stop=False),
                    waits=wl if j == 0 else [])
            b.op("tensor", lambda e, bk=bk: e.matmul(
                psp[bk][0][:, 0:256], obf[0:1, :], bvb[0:1, :], start=False, stop=True),
                inc=self.s_t)
            mk = self.s_t.n
            for kv in range(4):
                b.op("vector", lambda e, t16=t16, kv=kv, bk=bk: e.tensor_copy(
                    out=vt[:, t16 * 260 + 65 * kv:t16 * 260 + 65 * kv + 64],
                    in_=psp[bk][0][:, 64 * kv:64 * kv + 64]),
                    waits=[(self.s_t, mk)] if kv == 0 else [],
                    inc=self.s_v if kv == 3 else None)
            vev[t16] = self.s_v.n
        b.free(gpn)
        for h_, g in reversed(psp):
            b.free(g)
        b.free(gbv, gb4, gn4, gsqn, gsq, gqw, gwv, gwk, gwq, grk, grq)

        k2, gk2 = b.sbuf("atk2", [128, 4 * T], BF16)
        q2h, gq2 = b.sbuf("atq2", [128, 3 * T], BF16)
        pts, gpt = b.sbuf("atpt", [128, 16 * TC], BF16)
        rrs, grr = b.sbuf("atrr", [1, TC], F32)
        rbs, grb = b.sbuf("atrb", [64, TC], F32)
        wosb, gwo = b.sbuf("atwo", [128, 8 * 1024], BF16)
        for j in range(NJ):
            b.op("gpsimd", lambda e, j=j: e.dma_start(
                out=wosb[:, j * 1024:(j + 1) * 1024],
                in_=w["wo"][j * 128:(j + 1) * 128, :]),
                waits=[(self.s_v, self.s_v.n)] if j == 0 else [],
                inc=self.s_gld, k=16)
        wo_ld = self.s_gld.n
        pss = [b.psum(f"atps{i}", [128, 2 * TC]) for i in range(2)]
        psc = [b.psum(f"atpc{i}", [65, TC]) for i in range(2)]
        psb, gpsb = b.psum("atpb", [64, TC])

        kro_all = self.s_v.n
        for kv in range(4):
            b.op("gpsimd", lambda e, kv=kv: e.dma_start(
                out=k2[0:32, kv * T:(kv + 1) * T],
                in_=krot[32 * kv:32 * kv + 32, 0:T]),
                waits=[(self.s_v, kro_all)], inc=self.s_gld, k=16)
            b.op("gpsimd", lambda e, kv=kv: e.dma_start(
                out=k2[32:64, kv * T:(kv + 1) * T],
                in_=krot[32 * kv:32 * kv + 32, T:2 * T]), inc=self.s_gld, k=16)
        b.op("vector", lambda e: e.memset(k2[64:65, :], 1.0), inc=self.s_v)
        k2_g = self.s_gld.n
        k2_v = self.s_v.n

        qasm = {}
        avm = {}
        nrm = {}
        exp_pair = {}
        for h_ in range(NH):
            kv = h_ // 4
            hh = h_ % 4
            slot = h_ % 3
            wl = []
            if h_ >= 3:
                wl.append((self.s_t, avm[(h_ - 3, NTC - 1)]))
            b.op("gpsimd", lambda e, slot=slot, hh=hh, kv=kv: e.dma_start(
                out=q2h[0:32, slot * T:(slot + 1) * T],
                in_=qrot[32 * hh:32 * hh + 32, kv * T:(kv + 1) * T]),
                waits=wl, inc=self.sq3[slot], k=16)
            b.op("gpsimd", lambda e, slot=slot, hh=hh, kv=kv: e.dma_start(
                out=q2h[32:64, slot * T:(slot + 1) * T],
                in_=qrot[32 * hh:32 * hh + 32, (4 + kv) * T:(5 + kv) * T]),
                inc=self.sq3[slot], k=16)
            b.op("gpsimd", lambda e, slot=slot, h_=h_: e.dma_start(
                out=q2h[64:65, slot * T:(slot + 1) * T], in_=bnd[h_:h_ + 1, :]),
                inc=self.sq3[slot], k=16)
            qasm[h_] = self.sq3[slot].n
            for c in range(NTC):
                cs = slice(c * TC, (c + 1) * TC)
                nkt = 4 * c + 4
                npr = nkt // 2
                for pr in range(npr):
                    bk = pr % 2
                    wl = []
                    if c == 0 and pr == 0:
                        wl += [(self.sq3[slot], qasm[h_]), (self.s_v, k2_v),
                               (self.s_gld, k2_g)]
                    if (h_, c, pr - 2) in exp_pair:
                        wl.append((self.s_s, exp_pair[(h_, c, pr - 2)]))
                    elif pr < 2 and (c > 0 or h_ > 0):
                        ph, pc2 = (h_, c - 1) if c > 0 else (h_ - 1, NTC - 1)
                        lp = (4 * pc2 + 4) // 2 - 1
                        key = (ph, pc2, max(0, lp - (1 - pr)))
                        wl.append((self.s_s, exp_pair[key]))
                    for half in range(2):
                        kt = 2 * pr + half
                        diag = kt >= 4 * c
                        hsl = slice(half * TC, (half + 1) * TC)
                        if diag:
                            r = kt - 4 * c
                            b.op("tensor", lambda e, bk=bk, hsl=hsl, r=r: e.matmul(
                                pss[bk][0][:, hsl], idb[:, :],
                                cmk[:, r * TC:(r + 1) * TC], start=True, stop=False),
                                waits=wl if half == 0 else [])
                            wl = []
                        b.op("tensor", lambda e, bk=bk, hsl=hsl, kt=kt, kv=kv,
                             slot=slot, cs=cs, diag=diag: e.matmul(
                            pss[bk][0][:, hsl],
                            k2[0:65, kv * T + kt * 128:kv * T + (kt + 1) * 128],
                            q2h[0:65, slot * T:(slot + 1) * T][:, cs],
                            start=(not diag), stop=True),
                            waits=wl if half == 0 else [],
                            inc=self.s_t if half == 1 else None)
                        wl = []
                    mk = self.s_t.n
                    wl2 = [(self.s_t, mk)]
                    if pr == 0 and (h_, c - 1) in avm:
                        wl2.append((self.s_t, avm[(h_, c - 1)]))
                    if pr == 0 and h_ > 0 and c == 0:
                        wl2.append((self.s_t, avm[(h_ - 1, NTC - 1)]))
                    b.op("scalar", lambda e, bk=bk, pr=pr: e.activation(
                        out=pts[:, pr * 2 * TC:(pr + 1) * 2 * TC],
                        in_=pss[bk][0][:, :], func=AF.Exp),
                        waits=wl2, inc=self.s_s)
                    exp_pair[(h_, c, pr)] = self.s_s.n
                cbk = c % 2
                wl = [(self.s_s, exp_pair[(h_, c, npr - 1)])]
                if (h_, c - 2) in nrm:
                    wl.append((self.s_v, nrm[(h_, c - 2)]))
                if c < 2 and h_ > 0:
                    wl.append((self.s_v, nrm[(h_ - 1, NTC - 2 + c)]))
                for kt in range(nkt):
                    b.op("tensor", lambda e, kt=kt, kv=kv, cbk=cbk: e.matmul(
                        psc[cbk][0][:, :],
                        vt[:, kt * 260 + 65 * kv:kt * 260 + 65 * kv + 65],
                        pts[:, kt * TC:(kt + 1) * TC], start=(kt == 0),
                        stop=(kt == nkt - 1)),
                        waits=wl if kt == 0 else [],
                        inc=self.s_t if kt == nkt - 1 else None)
                avm[(h_, c)] = self.s_t.n
                b.op("scalar", lambda e, cbk=cbk: e.activation(
                    out=rrs[0:1, :], in_=psc[cbk][0][64:65, :], func=AF.Ln),
                    waits=[(self.s_t, avm[(h_, c)]), (self.s_t, self.s_t.n)],
                    inc=self.s_s)
                b.op("scalar", lambda e: e.activation(
                    out=rrs[0:1, :], in_=rrs[0:1, :], func=AF.Exp, scale=-1.0),
                    inc=self.s_s)
                rm = self.s_s.n
                b.op("tensor", lambda e: e.matmul(
                    psb[:, :], self.vec[0:1, 0:64], rrs[0:1, :], start=True,
                    stop=True), waits=[(self.s_s, rm)], inc=self.s_t)
                bm2 = self.s_t.n
                b.op("vector", lambda e: e.tensor_copy(out=rbs[:, :], in_=psb[:, :]),
                     waits=[(self.s_t, bm2)], inc=self.s_v)
                b.op("vector", lambda e, h_=h_, cbk=cbk, cs=cs: e.tensor_tensor(
                    out=ctxF[64 * (h_ % 2):64 * (h_ % 2) + 64,
                             (h_ // 2) * T:(h_ // 2 + 1) * T][:, cs],
                    in0=psc[cbk][0][0:64, :], in1=rbs[:, :], op=OP.mult),
                    inc=self.s_v)
                nrm[(h_, c)] = self.s_v.n
        all_heads_v = self.s_v.n
        b.free(gpsb)
        for h_, g in reversed(psc):
            b.free(g)
        for h_, g in reversed(pss):
            b.free(g)

        xre, gxr = b.sbuf("atxre", [128, 2 * TC], F32)
        wo_ps = [b.psum(f"atwops{i}", [128, TC]) for i in range(2)]
        xadd = {}
        for oc in range(NJ):
            for c in range(NTC):
                it = oc * NTC + c
                bk = it % 2
                wl = [(self.s_v, all_heads_v), (self.s_gld, wo_ld)] if it == 0 else []
                if it >= 2:
                    wl.append((self.s_v, xadd[it - 2]))
                wlx = [(self.s_v, all_heads_v)] if it < 2 else                     [(self.s_v, xadd[it - 2])]
                b.op("sync", lambda e, oc=oc, c=c, it=it: e.dma_start(
                    out=xre[:, (it % 2) * TC:(it % 2 + 1) * TC],
                    in_=xsrc[oc * 128:(oc + 1) * 128, c * TC:(c + 1) * TC]),
                    waits=wlx, inc=self.sx2[it % 2], k=16)
                xl = self.sx2[it % 2].n
                for j in range(NJ):
                    b.op("tensor", lambda e, j=j, oc=oc, c=c, bk=bk: e.matmul(
                        wo_ps[bk][0][:, :],
                        wosb[:, j * 1024 + oc * 128:j * 1024 + (oc + 1) * 128],
                        ctxF[:, j * T + c * TC:j * T + (c + 1) * TC],
                        start=(j == 0), stop=(j == NJ - 1)),
                        waits=wl if j == 0 else [],
                        inc=self.s_t if j == NJ - 1 else None)
                mk = self.s_t.n
                b.op("vector", lambda e, oc=oc, it=it, bk=bk: e.scalar_tensor_tensor(
                    out=xre[:, (it % 2) * TC:(it % 2 + 1) * TC],
                    in0=wo_ps[bk][0][:, :], scalar=self.vsl("bo", oc, 1),
                    in1=xre[:, (it % 2) * TC:(it % 2 + 1) * TC],
                    op0=OP.add, op1=OP.add),
                    waits=[(self.s_t, mk), (self.sx2[it % 2], xl)], inc=self.s_v)
                xadd[it] = self.s_v.n
                b.op("sync", lambda e, oc=oc, c=c, it=it: e.dma_start(
                    out=xdst[oc * 128:(oc + 1) * 128, c * TC:(c + 1) * TC],
                    in_=xre[:, (it % 2) * TC:(it % 2 + 1) * TC]),
                    waits=[(self.s_v, xadd[it])], inc=self.s_st, k=16)
        for h_, g in reversed(wo_ps):
            b.free(g)
        b.free(gxr)
        b.free(gwo, grb, grr, gpt, gq2, gk2)
        b.free(gsm, gob, gcm, gid, gcf, gbn, gvt, gkr, gqr, gxn)

    def emit_mamba(self, xsrc, xdst):
        b = self.b
        w = self.w
        xnroll, gxr0 = b.sbuf("mmxnr", [128, 2 * T], BF16)
        self.emit_ln(xsrc, xnroll, affine=("mg", "mb"), spill=self.xnspill,
                     affine_spill=self.xnmsp)
        b.barrier()
        b.free(gxr0)
        NJ16 = 16

        xpw, gxpw = b.sbuf("mmxpw", [128, 16 * 96], F32)
        dpw, gdpw = b.sbuf("mmdpw", [64, 2048], BF16)
        apt, gapt = b.sbuf("mmapt", [128, 256], F32)
        rsc, grsc = b.sbuf("mmrsc", [128, 2], F32)
        rssb, grss = b.sbuf("mmrss", [1, 2], F32)
        xnmC, gxnm = b.sbuf("mmxnc", [128, 8 * TC], BF16)
        R1, gR1 = b.sbuf("mmR1", [128, 32768], BF16)
        R1f = R1.bitcast(F32)
        R3, gR3 = b.sbuf("mmR3", [128, 8240], F32)
        tails, gtl = b.sbuf("mmtl", [128, 48], F32)
        sz, gsz = b.sbuf("mmsz", [128, 16 * TC], BF16)
        xcf, gxcf = b.sbuf("mmxcf", [128, 16 * TC], F32)
        y8, gy8 = b.sbuf("mmy8", [128, 8 * TC], F32)
        bcb, gbcb = b.sbuf("mmbcb", [128, 2 * TC], F32)
        da, gda = b.sbuf("mmda", [128, TC], F32)
        dbx, gdbx = b.sbuf("mmdbx", [128, TC], F32)
        hh, ghh = b.sbuf("mmhh", [128, TC], F32)
        gg2 = dbx
        hst, ghst = b.sbuf("mmhst", [128, 256], F32)
        bcs, gbcs = b.sbuf("mmbcs", [32, TC], F32)
        xdbl, gxd = b.sbuf("mmxd", [64, TC], BF16)
        btmp, gbt = b.sbuf("mmbt", [1, TC], F32)
        xre, gxre = b.sbuf("mmxre", [128, TC], F32)
        xns2, gxns = b.sbuf("mmxns", [128, TC], BF16)
        pp = [b.psum(f"mmpp{i}", [128, TC]) for i in range(2)]
        p96, gp96 = b.psum("mmp96", [96, TC])
        pdt, gpdt = b.psum("mmpdt", [128, TC])
        pbc, gpbc = b.psum("mmpbc", [128, TC])
        pout = [b.psum(f"mmpo{i}", [128, TC]) for i in range(2)]
        onesr = self.vec[0:1, 0:128]

        def ipw_v(j):
            return R1[:, j * 4096:(j + 1) * 4096]

        def u_v(j):
            return R1f[:, j * TC:(j + 1) * TC]

        def opw_v(j):
            return R1[:, 16384 + j * 1024:16384 + (j + 1) * 1024]

        def xp_v(j):
            return R3[:, j * 515:j * 515 + 515]

        def dt_v(j):
            return R3[:, j * TC:(j + 1) * TC]

        b.op("sync", lambda e: e.dma_start(out=rssb[0:1, 0:1], in_=w["rs"]),
             inc=self.s_ld, k=16)
        rld = self.s_ld.n
        b.op("tensor", lambda e: e.matmul(pbc[:, 0:1], onesr, rssb[0:1, 0:1],
                                          start=True, stop=True),
             waits=[(self.s_ld, rld)], inc=self.s_t)
        b.op("vector", lambda e: e.tensor_copy(out=rsc[:, 0:1], in_=pbc[:, 0:1]),
             waits=[self.s_t.at()], inc=self.s_v)
        rs_done = self.s_v.n
        b.op("scalar", lambda e: e.activation(out=apt[:, :], in_=self.vsl("alog"),
                                              func=AF.Exp), inc=self.s_s)
        b.op("vector", lambda e: e.tensor_scalar(
            out=apt[:, :], in0=apt[:, :], scalar1=-1.0, scalar2=None, op0=OP.mult),
            waits=[self.s_s.at()], inc=self.s_v)
        apt_done = self.s_v.n
        b.op("gpsimd", lambda e: e.dma_start(out=dpw[:, :], in_=w["dpw"]),
             inc=self.sq3[0], k=16)
        for j in range(NJ16):
            b.op("sync", lambda e, j=j: e.dma_start(
                out=xpw[:, j * 96:(j + 1) * 96], in_=w["xpw"][j * 128:(j + 1) * 128, :]),
                inc=self.s_ld, k=16)
        b.op("vector", lambda e: e.memset(tails[:, :], 0.0), inc=self.s_v)
        setup_v = self.s_v.n
        setup_g = self.sq3[0].n
        setup_l = self.s_ld.n

        prev_c1mm = 0
        prev_u_used = 0
        prev_opw_used = 0
        prev_tail = 0
        prev_sz_used = 0
        prev_hsave = {}
        ipw_ld = 0
        opw_ld = 0
        prev_scan = 0
        prev_xcf_used = 0
        self._mblk = 0
        self._pmark = {}

        for c in range(NTC):
            csl = slice(c * TC, (c + 1) * TC)
            wlw = [(self.s_v, prev_u_used), (self.s_t, prev_opw_used)]
            if c == 0:
                for j in range(NJ):
                    b.op("gpsimd", lambda e, j=j: e.dma_start(
                        out=ipw_v(j), in_=w["ipw"][j * 128:(j + 1) * 128, :]),
                        waits=wlw if j == 0 else [], inc=self.s_gld, k=16)
                    wlw = []
                ipw_ld = self.s_gld.n
                for j in range(NJ):
                    b.op("gpsimd", lambda e, j=j: e.dma_start(
                        out=self.ipwc[j * 128:(j + 1) * 128, :], in_=ipw_v(j)),
                        inc=self.sq3[2], k=16)
                ipwc_st = self.sq3[2].n
            else:
                for j in range(NJ):
                    b.op("sync", lambda e, j=j: e.dma_start(
                        out=ipw_v(j), in_=self.ipwc[j * 128:(j + 1) * 128, :]),
                        waits=(wlw + [(self.sq3[2], ipwc_st)]) if j == 0 else [],
                        inc=self.s_ld, k=16)
                    wlw = []
                ipw_ld = self.s_ld.n
            for j in range(NJ):
                b.op("sync", lambda e, j=j, c=c: e.dma_start(
                    out=xnmC[:, j * TC:(j + 1) * TC],
                    in_=self.xnmsp[j * 128:(j + 1) * 128, c * TC:(c + 1) * TC]),
                    waits=[(self.s_t, prev_c1mm), (self.s_t, prev_opw_used),
                           (self.s_sp2, self._affsp[NJ - 1])]
                    if j == 0 else [], inc=self.s_ld, k=16)
            xnm_ld = self.s_ld.n

            ev = {}
            for ec in range(32):
                bank = ec % 2
                wl = []
                if ec == 0:
                    wl = [(self.s_ld, xnm_ld), (self.s_v, setup_v)]
                    if c == 0:
                        wl.append((self.s_gld, ipw_ld))
                    else:
                        wl.append((self.s_ld, ipw_ld))
                if ec >= 2:
                    wl.append((self.s_s, ev[ec - 2]))
                if ec < 2 and c > 0:
                    wl.append((self.s_s, self._mm_ev_prev[ec]))
                for j in range(NJ):
                    b.op("tensor", lambda e, j=j, ec=ec, bank=bank: e.matmul(
                        pp[bank][0][:, :], ipw_v(j)[:, ec * 128:(ec + 1) * 128],
                        xnmC[:, j * TC:(j + 1) * TC],
                        start=(j == 0), stop=(j == NJ - 1)),
                        waits=wl if j == 0 else [],
                        inc=self.s_t if j == NJ - 1 else None)
                mk = self.s_t.n
                if ec < 16:
                    wl2 = [(self.s_t, mk)]
                    if c > 0 and ec < 2:
                        wl2.append((self.s_v, prev_tail))
                    b.op("scalar", lambda e, ec=ec, bank=bank: e.activation(
                        out=xp_v(ec)[:, 3:3 + TC], in_=pp[bank][0][:, :],
                        func=AF.Identity), waits=wl2, inc=self.s_s)
                else:
                    wl2 = [(self.s_t, mk)]
                    if ec < 18 and c > 0:
                        wl2.append((self.s_t, prev_sz_used))
                    b.op("scalar", lambda e, ec=ec, bank=bank: e.activation(
                        out=sz[:, (ec - 16) * TC:(ec - 15) * TC],
                        in_=pp[bank][0][:, :], func=AF.Silu), waits=wl2, inc=self.s_s)
                ev[ec] = self.s_s.n
            prev_c1mm = self.s_t.n
            self._mm_ev_prev = {0: ev[30], 1: ev[31]}

            cv = {}
            for j in range(NJ16):
                wl = [(self.s_s, ev[j])]
                if c > 0:
                    pass
                b.op("vector", lambda e, j=j: e.tensor_copy(
                    out=xp_v(j)[:, 0:3], in_=tails[:, j * 3:(j + 1) * 3]), waits=wl)
                b.op("vector", lambda e, j=j: e.tensor_scalar(
                    out=hh[:, 0:TC], in0=xp_v(j)[:, 0:TC],
                    scalar1=self.vsl("cw", 4 * j, 1), scalar2=self.vsl("convb", j, 1),
                    op0=OP.mult, op1=OP.add),
                    waits=([(self.s_s, cv[j - 1])] if j >= 1 else
                           [(self.s_p, self.s_p.n)]))
                for kq in range(1, 4):
                    b.op("vector", lambda e, j=j, kq=kq: e.scalar_tensor_tensor(
                        out=hh[:, 0:TC], in0=xp_v(j)[:, kq:kq + TC],
                        scalar=self.vsl("cw", 4 * j + kq, 1), in1=hh[:, 0:TC],
                        op0=OP.mult, op1=OP.add))
                b.op("vector", lambda e, j=j: e.tensor_copy(
                    out=tails[:, j * 3:(j + 1) * 3], in_=xp_v(j)[:, TC:TC + 3]),
                    inc=self.s_v)
                tm = self.s_v.n
                b.op("scalar", lambda e, j=j: e.activation(
                    out=xcf[:, j * TC:(j + 1) * TC], in_=hh[:, 0:TC], func=AF.Silu),
                    waits=[(self.s_v, tm), (self.s_v, prev_xcf_used)
                           if (c > 0 and j < 1) else (self.s_v, 0)], inc=self.s_s)
                cv[j] = self.s_s.n
            prev_tail = self.s_v.n

            for j in range(NJ16):
                b.op("tensor", lambda e, j=j: e.matmul(
                    p96[:, :], xpw[:, j * 96:(j + 1) * 96],
                    xcf[:, j * TC:(j + 1) * TC], start=(j == 0), stop=(j == NJ16 - 1)),
                    waits=([(self.s_s, cv[j])] if j < NJ16 else []) +
                          ([(self.s_ld, setup_l), (self.s_v, self._p96ev)]
                           if (j == 0 and c > 0) else
                           ([(self.s_ld, setup_l)] if j == 0 else [])),
                    inc=self.s_t if j == NJ16 - 1 else None)
            mk96 = self.s_t.n
            b.op("vector", lambda e: e.tensor_copy(out=bcs[:, :], in_=p96[64:96, :]),
                 waits=[(self.s_t, mk96)])
            b.op("vector", lambda e: e.tensor_copy(out=xdbl[:, :], in_=p96[0:64, :]),
                 inc=self.s_v)
            self._p96ev = self.s_v.n
            bc_done = self.s_v.n

            dtm = {}
            for j in range(NJ16):
                wl = [(self.s_v, bc_done), (self.sq3[0], setup_g)]
                if j > 0:
                    wl = [(self.s_s, dtm[j - 1])]
                b.op("tensor", lambda e, j=j: e.matmul(
                    pdt[:, :], dpw[0:64, j * 128:(j + 1) * 128], xdbl[:, :],
                    start=True, stop=True), waits=wl, inc=self.s_t)
                mk = self.s_t.n
                wl_abs = [(self.s_t, mk)]
                if j > 0:
                    wl_abs.append((self.s_v, self._dtadd))
                b.op("scalar", lambda e, j=j: e.activation(
                    out=da[:, :], in_=pdt[:, :], func=AF.Abs,
                    bias=self.vsl("dtb", j, 1)), waits=wl_abs)
                b.op("scalar", lambda e, j=j: e.activation(
                    out=dt_v(j)[:, :], in_=pdt[:, :], func=AF.Relu,
                    bias=self.vsl("dtb", j, 1)), inc=self.s_s)
                dtm[j] = self.s_s.n
                b.op("scalar", lambda e: e.activation(
                    out=da[:, :], in_=da[:, :], func=AF.Exp, scale=-1.0))
                b.op("scalar", lambda e: e.activation(
                    out=da[:, :], in_=da[:, :], func=AF.Ln,
                    bias=self.vsl("ones", 0, 1)), inc=self.s_s)
                lnm = self.s_s.n
                b.op("vector", lambda e, j=j: e.tensor_tensor(
                    out=dt_v(j)[:, :], in0=dt_v(j)[:, :], in1=da[:, :], op=OP.add),
                    waits=[(self.s_s, lnm)], inc=self.s_v)
                self._dtadd = self.s_v.n
                b.op("vector", lambda e, j=j: e.tensor_tensor(
                    out=u_v(j), in0=dt_v(j)[:, :], in1=xcf[:, j * TC:(j + 1) * TC],
                    op=OP.mult), inc=self.s_v)
            u_done = self.s_v.n

            wlw = [(self.s_v, u_done)]
            if c == 0:
                for j in range(NJ16):
                    b.op("gpsimd", lambda e, j=j: e.dma_start(
                        out=opw_v(j), in_=w["opw"][j * 128:(j + 1) * 128, :]),
                        waits=wlw if j == 0 else [], inc=self.s_gld, k=16)
                    wlw = []
                opw_ld = self.s_gld.n
                for j in range(NJ16):
                    b.op("gpsimd", lambda e, j=j: e.dma_start(
                        out=self.opwc[j * 128:(j + 1) * 128, :], in_=opw_v(j)),
                        inc=self.s_sp1, k=16)
                opwc_st = self.s_sp1.n
            else:
                for j in range(NJ16):
                    b.op("sync", lambda e, j=j: e.dma_start(
                        out=opw_v(j), in_=self.opwc[j * 128:(j + 1) * 128, :]),
                        waits=(wlw + [(self.s_sp1, opwc_st),
                                      (self.s_t, prev_opw_used)]) if j == 0 else [],
                        inc=self.s_ld, k=16)
                    wlw = []
                opw_ld = self.s_ld.n

            gate = {}
            for jh in range(2):
                for s_ in range(16):
                    wl = [(self.s_v, bc_done)]
                    b.op("gpsimd", lambda e, s_=s_: e.dma_start(
                        out=btmp[0:1, :], in_=bcs[s_:s_ + 1, :]),
                        waits=wl + [(self.s_t, self.s_t.n)], inc=self.sq3[1], k=16)
                    bt1 = self.sq3[1].n
                    b.op("tensor", lambda e: e.matmul(
                        pbc[:, :], onesr, btmp[0:1, :], start=True, stop=True),
                        waits=[(self.sq3[1], bt1), (self.s_v, self.s_v.n)],
                        inc=self.s_t)
                    b.op("vector", lambda e: e.tensor_copy(out=bcb[:, 0:TC],
                                                           in_=pbc[:, :]),
                         waits=[self.s_t.at()], inc=self.s_v)
                    bm1 = self.s_v.n
                    b.op("gpsimd", lambda e, s_=s_: e.dma_start(
                        out=btmp[0:1, :], in_=bcs[16 + s_:17 + s_, :]),
                        waits=[(self.s_t, self.s_t.n)], inc=self.sq3[1], k=16)
                    bt2 = self.sq3[1].n
                    b.op("tensor", lambda e: e.matmul(
                        pbc[:, :], onesr, btmp[0:1, :], start=True, stop=True),
                        waits=[(self.sq3[1], bt2), (self.s_v, bm1)], inc=self.s_t)
                    b.op("vector", lambda e: e.tensor_copy(out=bcb[:, TC:2 * TC],
                                                           in_=pbc[:, :]),
                         waits=[self.s_t.at()], inc=self.s_v)
                    for jj in range(8):
                        j = jh * 8 + jj
                        blk = j * 16 + s_
                        gb = self._mblk
                        self._mblk += 1
                        wl = [(self.s_v, prev_scan), (self.s_v, apt_done)]
                        if jh == 0 and s_ == 0 and jj == 0:
                            wl.append((self.s_v, u_done))
                        b.op("scalar", lambda e, j=j, blk=blk: e.activation(
                            out=da[:, :], in_=dt_v(j)[:, :], func=AF.Exp,
                            scale=apt[:, blk:blk + 1]), waits=wl, inc=self.s_s)
                        da_m = self.s_s.n
                        b.op("vector", lambda e, j=j: e.tensor_tensor(
                            out=dbx[:, :], in0=u_v(j), in1=bcb[:, 0:TC], op=OP.mult))
                        if c == 0:
                            b.op("vector", lambda e: e.tensor_tensor_scan(
                                out=hh[:, :], data0=da[:, :], data1=dbx[:, :],
                                initial=0.0, op0=OP.mult, op1=OP.add),
                                waits=[(self.s_s, da_m)])
                        else:
                            b.op("vector", lambda e, blk=blk: e.tensor_tensor_scan(
                                out=hh[:, :], data0=da[:, :], data1=dbx[:, :],
                                initial=hst[:, blk:blk + 1], op0=OP.mult, op1=OP.add),
                                waits=[(self.s_s, da_m)])
                        b.op("vector", lambda e, blk=blk: e.tensor_copy(
                            out=hst[:, blk:blk + 1], in_=hh[:, TC - 1:TC]))
                        b.op("vector", lambda e: e.tensor_tensor(
                            out=gg2[:, :], in0=hh[:, :], in1=bcb[:, TC:2 * TC],
                            op=OP.mult), inc=self.s_v)
                        prev_scan = self.s_v.n
                        if s_ == 0:
                            b.op("vector", lambda e, jj=jj: e.tensor_copy(
                                out=y8[:, jj * TC:(jj + 1) * TC], in_=gg2[:, :]),
                                inc=self.s_p)
                        else:
                            b.op("vector", lambda e, jj=jj: e.tensor_tensor(
                                out=y8[:, jj * TC:(jj + 1) * TC],
                                in0=y8[:, jj * TC:(jj + 1) * TC], in1=gg2[:, :],
                                op=OP.add), inc=self.s_p)
                        self._pmark[gb] = self.s_p.n
                for jj in range(8):
                    j = jh * 8 + jj
                    b.op("vector", lambda e, j=j, jj=jj: e.scalar_tensor_tensor(
                        out=y8[:, jj * TC:(jj + 1) * TC],
                        in0=xcf[:, j * TC:(j + 1) * TC], scalar=self.vsl("dd", j, 1),
                        in1=y8[:, jj * TC:(jj + 1) * TC], op0=OP.mult, op1=OP.add),
                        waits=[(self.s_p, self._pmark[self._mblk - 1])]
                        if jj == 0 else [])
                    b.op("vector", lambda e, j=j, jj=jj: e.tensor_tensor(
                        out=sz[:, j * TC:(j + 1) * TC],
                        in0=y8[:, jj * TC:(jj + 1) * TC],
                        in1=sz[:, j * TC:(j + 1) * TC], op=OP.mult), inc=self.s_v)
                    gate[j] = self.s_v.n
            prev_xcf_used = self.s_v.n

            oadd = {}
            stm = {}
            for oc in range(NJ):
                bank = oc % 2
                wl = [(self.s_v, gate[15])]
                if c == 0:
                    wl.append((self.s_gld, opw_ld))
                else:
                    wl.append((self.s_ld, opw_ld))
                if oc >= 2:
                    wl.append((self.s_v, oadd[oc - 2]))
                wlx = [(self.s_st, stm[oc - 1])] if oc >= 1 else [(self.s_v, gate[15])]
                b.op("sync", lambda e, oc=oc, c=c: e.dma_start(
                    out=xre[:, :],
                    in_=xsrc[oc * 128:(oc + 1) * 128, c * TC:(c + 1) * TC]),
                    waits=wlx, inc=self.s_x, k=16)
                b.op("sync", lambda e, oc=oc, c=c: e.dma_start(
                    out=xns2[:, :],
                    in_=self.xnspill[oc * 128:(oc + 1) * 128, c * TC:(c + 1) * TC]),
                    inc=self.s_x, k=16)
                xl = self.s_x.n
                for j in range(NJ16):
                    b.op("tensor", lambda e, j=j, oc=oc, bank=bank: e.matmul(
                        pout[bank][0][:, :],
                        opw_v(j)[:, oc * 128:(oc + 1) * 128],
                        sz[:, j * TC:(j + 1) * TC],
                        start=(j == 0), stop=(j == NJ16 - 1)),
                        waits=wl if j == 0 else [],
                        inc=self.s_t if j == NJ16 - 1 else None)
                mk = self.s_t.n
                b.op("vector", lambda e, oc=oc, bank=bank: e.scalar_tensor_tensor(
                    out=xre[:, :],
                    in0=pout[bank][0][:, :], scalar=rsc[:, 0:1],
                    in1=xre[:, :],
                    op0=OP.mult, op1=OP.add),
                    waits=[(self.s_t, mk), (self.s_x, xl), (self.s_v, rs_done)])
                b.op("vector", lambda e, oc=oc: e.tensor_tensor(
                    out=xre[:, :],
                    in0=xre[:, :],
                    in1=xns2[:, :], op=OP.add),
                    inc=self.s_v)
                oadd[oc] = self.s_v.n
                b.op("sync", lambda e, oc=oc, c=c: e.dma_start(
                    out=xdst[oc * 128:(oc + 1) * 128, c * TC:(c + 1) * TC],
                    in_=xre[:, :]),
                    waits=[(self.s_v, oadd[oc])], inc=self.s_st, k=16)
                stm[oc] = self.s_st.n
            prev_opw_used = self.s_t.n
            prev_u_used = self.s_v.n
            prev_sz_used = self.s_t.n

        for h_, g in reversed(pout):
            b.free(g)
        b.free(gpbc, gpdt, gp96)
        for h_, g in reversed(pp):
            b.free(g)
        b.free(gxns, gxre, gbt, gxd, gbcs, ghst, ghh, gdbx, gda, gbcb,
               gy8, gxcf, gsz, gtl, gR3, gR1, gxnm, grss, grsc, gapt, gdpw,
               gxpw)


def host_prep(inp):
    sh = {}
    sh["vec"] = build_vec_blob(inp)
    sh["w1T"] = np.ascontiguousarray(np.asarray(inp["L0_ffn_w1"]).T)
    sh["w2T"] = np.ascontiguousarray(np.asarray(inp["L0_ffn_w2"]).T)
    sh["wq"] = perm_q(np.asarray(inp["L0_attn_Wq"]))
    sh["wk"] = perm_k(np.asarray(inp["L0_attn_Wk"]))
    sh["wv"] = np.ascontiguousarray(np.asarray(inp["L0_attn_Wv"]))
    sh["bv"] = np.asarray(inp["L0_attn_bv"]).reshape(1, 256)
    sh["wo"] = np.ascontiguousarray(np.asarray(inp["L0_attn_Wo"]))
    j = np.arange(32, dtype=np.float64)
    inv = 10000.0 ** (-2.0 * j / 64.0)
    tpos = np.arange(T, dtype=np.float64)
    fr = tpos[None, :] * inv[:, None]
    fr = np.tile(fr, (4, 1))
    cos, sin = np.cos(fr), np.sin(fr)
    sh["rotq"] = np.concatenate([cos / 8.0, sin / 8.0], axis=1).astype(np.float32)
    sh["rotk"] = np.concatenate([cos, sin], axis=1).astype(np.float32)
    masks = []
    kk = np.arange(128)[:, None]
    qq = np.arange(TC)[None, :]
    for r in range(4):
        masks.append(np.where(kk <= qq - 128 * r, 0.0, -30000.0))
    sh["cmask"] = np.concatenate(masks, axis=1).astype(BF16NP)
    sh["identb"] = np.eye(128, dtype=BF16NP)
    sh["onesbf"] = np.ones((1, 128), dtype=BF16NP)
    sh["ipw"] = np.ascontiguousarray(np.asarray(inp["L1_attn_in_proj_w"]).T)
    sh["xpw"] = np.ascontiguousarray(np.asarray(inp["L1_attn_x_proj_w"]).T)
    sh["dpw"] = np.ascontiguousarray(np.asarray(inp["L1_attn_dt_proj_w"]).T)
    sh["opw"] = np.ascontiguousarray(np.asarray(inp["L1_attn_out_proj_w"]).T)
    sh["rs"] = np.asarray(inp["L1_attn_res_scale"]).reshape(1, 1).astype(np.float32)
    sh["q1"] = np.ascontiguousarray(np.asarray(inp["L1_ffn_fc1_q"], dtype=np.int32))
    sh["q2"] = np.ascontiguousarray(np.asarray(inp["L1_ffn_fc2_q"], dtype=np.int32))
    sh["sc1"] = np.ascontiguousarray(np.asarray(inp["L1_ffn_fc1_scales"]))
    sh["sc2"] = np.ascontiguousarray(np.asarray(inp["L1_ffn_fc2_scales"]))
    rs8 = np.zeros((8, 8 * 128), np.float32)
    for g in range(8):
        rs8[g, g * 128:(g + 1) * 128] = 1.0
    sh["rowsel8"] = rs8
    rs32 = np.zeros((32, 32 * 128), np.float32)
    for g in range(32):
        rs32[g, g * 128:(g + 1) * 128] = 1.0
    sh["rowsel32"] = rs32
    return sh


_PROG_CACHE = {}


def get_prog(stages):
    key = tuple(stages)
    if key not in _PROG_CACHE:
        k = Kern(list(stages))
        k.emit()
        _PROG_CACHE[key] = k
    return _PROG_CACHE[key]


def kernel(**inp):
    sh = host_prep(inp)
    x = np.asarray(inp["L0_attn_x"])
    xcore = [np.ascontiguousarray(x[b].T) for b in range(8)]
    used_keys = None
    for stages in STAGE_SETS:
        k = get_prog(stages)
        names = set()
        for nm in list(k.w.keys()) + ["vec"]:
            names.add(nm)
        in_maps = []
        for b_ in range(8):
            m = {nm: sh[nm] for nm in names}
            m["xin"] = xcore[b_]
            in_maps.append(m)
        res = run_bass_kernel_spmd(k.nc, in_maps, core_ids=list(range(8)))
        xcore = [np.ascontiguousarray(res.results[b_]["xout"]) for b_ in range(8)]
    out = np.stack([xc.T for xc in xcore]).astype(np.float32)
    return np.ascontiguousarray(out)
